# version 10
# baseline (speedup 1.0000x reference)
import numpy as np
import ml_dtypes
from contextlib import ExitStack

import concourse.bass as bass
import concourse.mybir as mybir
from concourse.bass_utils import run_bass_kernel_spmd

BF = ml_dtypes.bfloat16
F32 = mybir.dt.float32
BF16 = mybir.dt.bfloat16
AF = mybir.ActivationFunctionType
ALU = mybir.AluOpType
AX = mybir.AxisListType

T = 2048
NT = 16
D = 1024
GROUPS = [[0, 1], [2, 3], [4, 5], [6, 7]]


class Sem:
    def __init__(self, h):
        self.h = h
        self.n = 0

    def inc(self, inst, d=1):
        inst.then_inc(self.h, d)
        self.n += d
        return self.n


class Builder:
    def __init__(self, stages=4):
        self.stages = stages
        nc = self.nc = bass.Bass("TRN2", target_bir_lowering=False)
        di = lambda n, s, dt: nc.dram_tensor(n, s, dt, kind="ExternalInput").ap()
        self.x_in = di("x", [T, D], F32)
        self.wq_a = di("wq_a", [D, D], F32)
        self.wk_a = di("wk_a", [D, D], F32)
        self.wv_a = di("wv_a", [D, D], F32)
        self.wo_a = di("wo_a", [D, D], F32)
        self.boa_c = di("boa_c", [128, 8], F32)
        self.gamma_v = di("gamma_v", [D], F32)
        self.beta_v = di("beta_v", [D], F32)
        self.wqt_b = di("wqt_b", [D, D], BF16)
        self.wkt_b = di("wkt_b", [D, D], BF16)
        self.wvt_b = di("wvt_b", [D, D], BF16)
        self.wot_b = di("wot_b", [D, D], BF16)
        self.bqb_c = di("bqb_c", [128, 8], F32)
        self.bkb_c = di("bkb_c", [128, 8], F32)
        self.bvb_v = di("bvb_v", [D], F32)
        self.bob_c = di("bob_c", [128, 8], F32)
        self.windup = di("windup", [D, 128], F32)
        self.woutdup = di("woutdup", [128, D], F32)
        self.ffncols = di("ffncols", [128, 48], F32)
        self.ffnmisc = di("ffnmisc", [128, 2], F32)
        self.boutc = di("boutc", [128, 8], F32)
        self.cos_i = di("cosT", [128, T], BF16)
        self.sin_i = di("sinS", [128, T], BF16)
        self.idb_i = di("idb", [128, 128], BF16)
        self.idf_i = di("idf", [128, 128], F32)
        self.ones_i = di("onesb", [128, 128], BF16)
        self.perm_i = di("permb", [128, 128], BF16)
        self.eps_i = di("epsv", [4], F32)
        self.out_d = nc.dram_tensor("out", [T, D], F32, kind="ExternalOutput").ap()

        self.kst_d = nc.dram_tensor("kst_d", [128, 16, 1024], BF16).ap()
        self.zk_d = nc.dram_tensor("zk_d", [128, 8], F32).ap()
        self.zk_o = nc.dram_tensor("zk_o", [2, 128, 8], F32).ap()
        self.c_d = nc.dram_tensor("c_d", [2, 128, 8, 512], BF16).ap()
        self.c_o = nc.dram_tensor("c_o", [2, 2, 128, 8, 512], BF16).ap()
        self.kv_d = nc.dram_tensor("kv_d", [2, T * D], BF16).ap()
        self.kvK_o = nc.dram_tensor("kvK_o", [2, T * D], BF16).ap()
        self.kvV_o = nc.dram_tensor("kvV_o", [2, T * D], BF16).ap()

    def barrier(self):
        tgt = self.s_bar.n + 5
        self.s_bar.n = tgt
        with self.nc.Block() as block:
            for name in ("sync", "tensor", "vector", "scalar", "gpsimd"):
                def body(e, tgt=tgt):
                    e.drain()
                    e.sem_inc(self.s_bar.h, 1)
                    e.wait_ge(self.s_bar.h, tgt)
                getattr(block, name)(body)

    def build(self):
        nc = self.nc
        with ExitStack() as top:
            sb = lambda *a: top.enter_context(nc.sbuf_tensor(*a))
            mksem = lambda n: Sem(top.enter_context(nc.semaphore(n)))
            self.X = sb("X", [128, NT, D], F32)
            self.FM = sb("FM", [128, 8, T], BF16)
            self.WA = sb("WA", [128, 8, 1024], BF16)
            self.WB = sb("WB", [128, 8, 1024], BF16)
            self.QB = sb("QB", [128, 8, T], BF16)
            self.BOA = sb("BOA", [128, 8], F32)
            self.BQB = sb("BQB", [128, 8], F32)
            self.BKB = sb("BKB", [128, 8], F32)
            self.BOB = sb("BOB", [128, 8], F32)
            self.BOUT = sb("BOUT", [128, 8], F32)
            self.IDB = sb("IDB", [128, 128], BF16)
            self.IDF = sb("IDF", [128, 128], F32)
            self.ONES = sb("ONES", [128, 128], BF16)
            self.EPS = sb("EPS", [128, 4], F32)
            self.TS = sb("TS", [128, NT, 4], F32)
            self.ZKS = sb("ZKS", [128, 8, 6], F32)
            self.ZKG = sb("ZKG", [128, 8, 2], F32)

            for s in ("s_x", "s_w", "s_pe", "s_act", "s_ve", "s_ve2", "s_gp",
                      "s_dk", "s_cc", "s_st", "s_bar", "s_pb", "s_ep", "s_kv"):
                setattr(self, s, mksem(s))

            self.p0_loads()
            self.ln_phase(first=True, keep_stats=True)
            self.barrier()
            st = self.stages
            if st >= 1:
                self.phaseA_K()
                self.barrier()
                self.phaseA_VC()
                self.phaseA_QAttnOut()
                self.barrier()
            if st == 15 or st >= 2:
                self.ln_phase(first=False, keep_stats=False)
                self.barrier()
            if st >= 2 and st != 15:
                self.ffn_phase()
                self.barrier()
            if st >= 3 and st != 15:
                self.ln_phase(first=False, keep_stats=False)
                self.barrier()
                self.phaseB_proj()
                self.barrier()
                self.phaseB_attn()
                self.barrier()
            if st >= 4 and st != 15:
                self.ln_phase(first=False, keep_stats=False)
                self.barrier()
                self.ffn_phase()
                self.barrier()
            self.p9_store()
        return nc

    def p0_loads(self):
        nc = self.nc
        with nc.Block() as block:
            def sync_body(s):
                b = lambda v, w: v[None, :].broadcast_to([128, w])
                s.dma_start(out=self.EPS[:, :], in_=b(self.eps_i, 4)).then_inc(self.s_w.h, 16)
                s.dma_start(out=self.BOA[:, :], in_=self.boa_c).then_inc(self.s_w.h, 16)
                s.dma_start(out=self.BQB[:, :], in_=self.bqb_c).then_inc(self.s_w.h, 16)
                s.dma_start(out=self.BKB[:, :], in_=self.bkb_c).then_inc(self.s_w.h, 16)
                s.dma_start(out=self.BOB[:, :], in_=self.bob_c).then_inc(self.s_w.h, 16)
                s.dma_start(out=self.BOUT[:, :], in_=self.boutc).then_inc(self.s_w.h, 16)
                s.dma_start(out=self.IDB[:, :], in_=self.idb_i).then_inc(self.s_w.h, 16)
                s.dma_start(out=self.IDF[:, :], in_=self.idf_i).then_inc(self.s_w.h, 16)
                s.dma_start(out=self.ONES[:, :], in_=self.ones_i).then_inc(self.s_w.h, 16)
                self.s_w.n += 9 * 16
                xr = self.x_in.rearrange("(i p) d -> p i d", p=128)
                for ch in range(4):
                    s.dma_start(out=self.X[:, ch * 4:(ch + 1) * 4, :],
                                in_=xr[:, ch * 4:(ch + 1) * 4, :]).then_inc(self.s_x.h, 16)
                self.s_x.n += 64
            block.sync(sync_body)

            def gp_body(g):
                g.dma_start(out=self.WA[:, :, :],
                            in_=self.wk_a.rearrange("(k p) o -> p k o", p=128)).then_inc(self.s_w.h, 16)
                g.dma_start(out=self.WB[:, :, :],
                            in_=self.wv_a.rearrange("(k p) o -> p k o", p=128)).then_inc(self.s_w.h, 16)
                self.s_w.n += 32
            block.gpsimd(gp_body)
        self.W0 = self.s_w.n

    def ln_phase(self, first, keep_stats):
        nc = self.nc
        self._ln_ct = getattr(self, "_ln_ct", 0) + 1
        u = f"{self._ln_ct}"
        with ExitStack() as ctx:
            PSB = [ctx.enter_context(nc.psum_tensor(f"lnp{u}_{j}", [128, 1024], BF16)) for j in range(2)]
            NT32 = ctx.enter_context(nc.sbuf_tensor(f"lnNT32{u}", [128, 2, D], F32))
            NTB = ctx.enter_context(nc.sbuf_tensor(f"lnNTB{u}", [128, 2, D], BF16))
            SQD = ctx.enter_context(nc.sbuf_tensor(f"lnSQD{u}", [128, D], BF16))
            b_ve, b_ve2, b_act, b_pe, b_pb, b_gp = (self.s_ve.n, self.s_ve2.n, self.s_act.n,
                                                    self.s_pe.n, self.s_pb.n, self.s_gp.n)
            TS, X, EPS, FM, IDB = self.TS, self.X, self.EPS, self.FM, self.IDB
            with nc.Block() as block:
                def vec(v):
                    for i in range(NT):
                        if first:
                            v.wait_ge(self.s_x.h, 16 * (i // 4 + 1))
                        v.tensor_reduce(TS[:, i, 0:1], X[:, i, :], AX.X, ALU.add).then_inc(self.s_ve.h, 1)
                        self.s_ve.n += 1
                        v.wait_ge(self.s_ve.h, self.s_ve.n)
                        v.tensor_scalar(TS[:, i, 0:1], TS[:, i, 0:1], 1.0 / D, None,
                                        ALU.mult).then_inc(self.s_ve.h, 1)
                        self.s_ve.n += 1
                        if i >= 1:
                            k = i - 1
                            v.wait_ge(self.s_act.h, b_act + 2 * (k + 1))
                            v.reciprocal(TS[:, k, 3:4], TS[:, k, 2:3]).then_inc(self.s_ve2.h, 1)
                            self.s_ve2.n += 1
                        if i >= 2:
                            k = i - 2
                            v.wait_ge(self.s_pe.h, b_pe + (k + 1))
                            v.tensor_copy(FM[:, :, k * 128:(k + 1) * 128],
                                          PSB[k % 2][:, :].rearrange("p (c j) -> p c j", j=128)
                                          ).then_inc(self.s_pb.h, 1)
                            self.s_pb.n += 1
                    for k in (NT - 1,):
                        v.wait_ge(self.s_act.h, b_act + 2 * (k + 1))
                        v.reciprocal(TS[:, k, 3:4], TS[:, k, 2:3]).then_inc(self.s_ve2.h, 1)
                        self.s_ve2.n += 1
                    for k in (NT - 2, NT - 1):
                        v.wait_ge(self.s_pe.h, b_pe + (k + 1))
                        v.tensor_copy(FM[:, :, k * 128:(k + 1) * 128],
                                      PSB[k % 2][:, :].rearrange("p (c j) -> p c j", j=128)
                                      ).then_inc(self.s_pb.h, 1)
                        self.s_pb.n += 1
                block.vector(vec)

                def gp(g):
                    for i in range(NT):
                        r = i % 2
                        if i >= 2:
                            g.wait_ge(self.s_act.h, b_act + 2 * (i - 2) + 1)
                        g.wait_ge(self.s_ve.h, b_ve + 2 * (i + 1))
                        g.tensor_scalar(NT32[:, r, :], X[:, i, :], TS[:, i, 0:1], None,
                                        ALU.subtract).then_inc(self.s_gp.h, 1)
                        self.s_gp.n += 1
                        g.wait_ge(self.s_ve2.h, b_ve2 + i + 1)
                        if i >= 2:
                            g.wait_ge(self.s_pe.h, b_pe + (i - 1))
                        g.tensor_scalar(NTB[:, r, :], NT32[:, r, :], TS[:, i, 3:4], None,
                                        ALU.mult).then_inc(self.s_gp.h, 1)
                        self.s_gp.n += 1
                block.gpsimd(gp)

                def sca(sc):
                    if first:
                        sc.wait_ge(self.s_w.h, self.W0)
                    for i in range(NT):
                        r = i % 2
                        sc.wait_ge(self.s_gp.h, b_gp + 2 * i + 1)
                        sc.activation(SQD[:, :], NT32[:, r, :], AF.Square,
                                      accum_out=TS[:, i, 1:2]).then_inc(self.s_act.h, 1)
                        self.s_act.n += 1
                        sc.wait_ge(self.s_act.h, self.s_act.n)
                        sc.activation(TS[:, i, 2:3], TS[:, i, 1:2], AF.Sqrt,
                                      bias=EPS[:, 0:1], scale=1.0 / D).then_inc(self.s_act.h, 1)
                        self.s_act.n += 1
                block.scalar(sca)

                def ten(t):
                    for i in range(NT):
                        r = i % 2
                        t.wait_ge(self.s_gp.h, b_gp + 2 * (i + 1))
                        if i >= 2:
                            t.wait_ge(self.s_pb.h, b_pb + (i - 1))
                        for j in range(8):
                            inst = t.transpose(PSB[r][:, j * 128:(j + 1) * 128],
                                               NTB[:, r, j * 128:(j + 1) * 128], IDB[:, :])
                        inst.then_inc(self.s_pe.h, 1)
                        self.s_pe.n += 1
                block.tensor(ten)
        self.barrier()

    def phaseA_K(self):
        nc = self.nc
        with ExitStack() as ctx:
            PK = [ctx.enter_context(nc.psum_tensor(f"pk{j}", [128, 512], F32)) for j in range(2)]
            PSB = [ctx.enter_context(nc.psum_tensor(f"pkb{j}", [128, 512], BF16)) for j in range(2)]
            KEXPP = ctx.enter_context(nc.sbuf_tensor("KEXPP", [128, 2, 512], BF16))
            KSTS = ctx.enter_context(nc.sbuf_tensor("KSTS", [128, 4, 1024], BF16))
            b_pe, b_act, b_ve, b_pb, b_dk = (self.s_pe.n, self.s_act.n, self.s_ve.n,
                                             self.s_pb.n, self.s_dk.n)
            WA, FM, ZKS, ZKG, IDB = self.WA, self.FM, self.ZKS, self.ZKG, self.IDB

            def pe_proj_idx(n):
                return b_pe + (1 if n == 0 else 2 * n)

            def pe_tr_idx(m):
                return b_pe + (2 * m + 3 if m < 31 else 64)

            with nc.Block() as block:
                def ten(t):
                    n = 0
                    for tg in range(4):
                        for oc in range(8):
                            r = n % 2
                            if n >= 2:
                                t.wait_ge(self.s_act.h, b_act + (n - 1))
                            for k in range(8):
                                inst = t.matmul(PK[r][:, :], WA[:, k, oc * 128:(oc + 1) * 128],
                                                FM[:, k, tg * 512:(tg + 1) * 512],
                                                start=(k == 0), stop=(k == 7))
                            inst.then_inc(self.s_pe.h, 1)
                            self.s_pe.n += 1
                            if n >= 1:
                                m = n - 1
                                rr = m % 2
                                t.wait_ge(self.s_act.h, b_act + (m + 1))
                                if m >= 2:
                                    t.wait_ge(self.s_pb.h, b_pb + (m - 1))
                                for sub in range(4):
                                    inst = t.transpose(PSB[rr][:, sub * 128:(sub + 1) * 128],
                                                       KEXPP[:, rr, sub * 128:(sub + 1) * 128],
                                                       IDB[:, :])
                                inst.then_inc(self.s_pe.h, 1)
                                self.s_pe.n += 1
                            n += 1
                    m = 31
                    rr = m % 2
                    t.wait_ge(self.s_act.h, b_act + 32)
                    t.wait_ge(self.s_pb.h, b_pb + 30)
                    for sub in range(4):
                        inst = t.transpose(PSB[rr][:, sub * 128:(sub + 1) * 128],
                                           KEXPP[:, rr, sub * 128:(sub + 1) * 128], IDB[:, :])
                    inst.then_inc(self.s_pe.h, 1)
                    self.s_pe.n += 1
                block.tensor(ten)

                def sca(sc):
                    n = 0
                    for tg in range(4):
                        for oc in range(8):
                            r = n % 2
                            sc.wait_ge(self.s_pe.h, pe_proj_idx(n))
                            sc.activation(KEXPP[:, r, :], PK[r][:, :], AF.Exp,
                                          accum_out=ZKS[:, oc, tg:tg + 1]).then_inc(self.s_act.h, 1)
                            self.s_act.n += 1
                            n += 1
                block.scalar(sca)

                def vec(v):
                    m = 0
                    for tg in range(4):
                        for oc in range(8):
                            rr = m % 2
                            v.wait_ge(self.s_pe.h, pe_tr_idx(m))
                            if tg >= 1:
                                v.wait_ge(self.s_dk.h, b_dk + 16 * tg)
                            v.tensor_copy(KSTS[:, :, oc * 128:(oc + 1) * 128],
                                          PSB[rr][:, :].rearrange("p (s j) -> p s j", j=128)
                                          ).then_inc(self.s_pb.h, 1)
                            self.s_pb.n += 1
                            m += 1
                    v.wait_ge(self.s_act.h, b_act + 32)
                    v.tensor_reduce(ZKS[:, :, 4:5], ZKS[:, :, 0:4], AX.X, ALU.add
                                    ).then_inc(self.s_ve.h, 1)
                    self.s_ve.n += 1
                block.vector(vec)

                def syn(s):
                    for tg in range(4):
                        s.wait_ge(self.s_pb.h, b_pb + 8 * (tg + 1))
                        s.dma_start(out=self.kst_d[:, tg * 4:(tg + 1) * 4, :],
                                    in_=KSTS[:, :, :]).then_inc(self.s_dk.h, 16)
                        self.s_dk.n += 16
                    s.wait_ge(self.s_ve.h, b_ve + 1)
                    s.dma_start(out=self.zk_d, in_=ZKS[:, :, 4],
                                allow_slow_non_contiguous=True).then_inc(self.s_cc.h, 16)
                    self.s_cc.n += 16
                block.sync(syn)

                def gp(g):
                    g.wait_ge(self.s_cc.h, self.s_cc.n)
                    g.collective_compute("AllGather", ALU.bypass, replica_groups=GROUPS,
                                         ins=[self.zk_d], outs=[self.zk_o]).then_inc(self.s_cc.h, 1)
                    self.s_cc.n += 1
                    g.wait_ge(self.s_cc.h, self.s_cc.n)
                    g.dma_start(out=ZKS[:, :, 4], in_=self.zk_o[0],
                                allow_slow_non_contiguous=True).then_inc(self.s_cc.h, 16)
                    g.dma_start(out=ZKS[:, :, 5], in_=self.zk_o[1],
                                allow_slow_non_contiguous=True).then_inc(self.s_cc.h, 16)
                    self.s_cc.n += 32
                    g.wait_ge(self.s_pe.h, b_pe + 62)
                    g.dma_start(out=self.WA[:, :, :],
                                in_=self.wq_a.rearrange("(k p) o -> p k o", p=128)
                                ).then_inc(self.s_w.h, 16)
                    self.s_w.n += 16
                block.gpsimd(gp)
            self.WQ_A = self.s_w.n
            zk_in = self.s_cc.n

            with nc.Block() as block:
                def vec2(v):
                    v.wait_ge(self.s_cc.h, zk_in)
                    v.tensor_add(ZKG[:, :, 0], ZKS[:, :, 4], ZKS[:, :, 5]).then_inc(self.s_ve.h, 1)
                    self.s_ve.n += 1
                    v.wait_ge(self.s_ve.h, self.s_ve.n)
                    v.reciprocal(ZKG[:, :, 1], ZKG[:, :, 0]).then_inc(self.s_ve.h, 1)
                    self.s_ve.n += 1
                block.vector(vec2)
            self.RZK_READY = self.s_ve.n

    def phaseA_VC(self):
        nc = self.nc
        with ExitStack() as ctx:
            VH = ctx.enter_context(nc.sbuf_tensor("VH", [128, NT, 512], BF16))
            KSTIN = ctx.enter_context(nc.sbuf_tensor("KSTIN", [128, 2, 2, 1024], BF16))
            PS = ctx.enter_context(nc.psum_tensor("avcps", [128, 8, 512], F32))
            FM, WB, ZKG = self.FM, self.WB, self.ZKG
            p0, v0, dk0 = self.s_pe.n, self.s_ve.n, self.s_dk.n
            with nc.Block() as block:
                def ten(t):
                    for h in range(2):
                        peh, veh, dkh = p0 + 32 * h, v0 + 24 * h, dk0 + 128 * h
                        for i in range(NT):
                            r = i % 2
                            t.wait_ge(self.s_ve.h, veh + max(i - 1, 0))
                            for k in range(8):
                                inst = t.matmul(PS[:, r, :], FM[:, k, i * 128:(i + 1) * 128],
                                                WB[:, k, h * 512:(h + 1) * 512],
                                                start=(k == 0), stop=(k == 7))
                            inst.then_inc(self.s_pe.h, 1)
                            self.s_pe.n += 1
                        t.wait_ge(self.s_ve.h, veh + 16)
                        for i in range(NT):
                            pc, sub = i // 2, i % 2
                            t.wait_ge(self.s_dk.h, dkh + 16 * (pc + 1))
                            for cch in range(8):
                                inst = t.matmul(PS[:, cch, :],
                                                KSTIN[:, pc % 2, sub, cch * 128:(cch + 1) * 128],
                                                VH[:, i, :],
                                                start=(i == 0), stop=(i == NT - 1))
                            inst.then_inc(self.s_pe.h, 1)
                            self.s_pe.n += 1
                block.tensor(ten)

                def vec(v):
                    for h in range(2):
                        peh, veh = p0 + 32 * h, v0 + 24 * h
                        for i in range(NT):
                            r = i % 2
                            v.wait_ge(self.s_pe.h, peh + i + 1)
                            v.tensor_copy(VH[:, i, :], PS[:, r, :]).then_inc(self.s_ve.h, 1)
                            self.s_ve.n += 1
                        v.wait_ge(self.s_pe.h, peh + 32)
                        v.wait_ge(self.s_ve.h, self.RZK_READY)
                        for cch in range(8):
                            v.tensor_scalar(self.QB[:, cch, h * 512:(h + 1) * 512],
                                            PS[:, cch, :], ZKG[:, cch, 1:2], None,
                                            ALU.mult).then_inc(self.s_ve.h, 1)
                            self.s_ve.n += 1
                block.vector(vec)

                def syn(s):
                    for h in range(2):
                        peh, dkh = p0 + 32 * h, dk0 + 128 * h
                        for pc in range(8):
                            if pc >= 2:
                                s.wait_ge(self.s_pe.h, peh + 16 + (pc - 1) * 2)
                            elif h == 1:
                                s.wait_ge(self.s_pe.h, p0 + 32)
                            s.dma_start(out=KSTIN[:, pc % 2, :, :],
                                        in_=self.kst_d[:, pc * 2:(pc + 1) * 2, :]
                                        ).then_inc(self.s_dk.h, 16)
                            self.s_dk.n += 16
                    for h in range(2):
                        s.wait_ge(self.s_ve.h, v0 + 24 * h + 24)
                        s.dma_start(out=self.c_d[h],
                                    in_=self.QB[:, :, h * 512:(h + 1) * 512]
                                    ).then_inc(self.s_cc.h, 16)
                        self.s_cc.n += 16
                block.sync(syn)

                ship_t = self.s_cc.n

                def gp(g):
                    g.wait_ge(self.s_cc.h, ship_t - 16)
                    g.collective_compute("AllGather", ALU.bypass, replica_groups=GROUPS,
                                         ins=[self.c_d[0]], outs=[self.c_o[0]]).then_inc(self.s_cc.h, 1)
                    self.s_cc.n += 1
                    g.wait_ge(self.s_cc.h, ship_t)
                    g.collective_compute("AllGather", ALU.bypass, replica_groups=GROUPS,
                                         ins=[self.c_d[1]], outs=[self.c_o[1]]).then_inc(self.s_cc.h, 1)
                    self.s_cc.n += 1
                    g.dma_start(out=self.WB[:, :, :],
                                in_=self.wo_a.rearrange("(k p) o -> p k o", p=128)
                                ).then_inc(self.s_w.h, 16)
                    self.s_w.n += 16
                block.gpsimd(gp)
            self.C_AG0 = ship_t + 1
            self.C_AG1 = ship_t + 2
            self.WO_A = self.s_w.n


    def phaseA_QAttnOut(self):
        nc = self.nc
        with ExitStack() as ctx:
            esb = lambda *a: ctx.enter_context(nc.sbuf_tensor(*a))
            eps_ = lambda *a: ctx.enter_context(nc.psum_tensor(*a))
            CC = self.QB[:, :, 0:1024]
            QEX = self.QB[:, :, 1024:2048]
            RZQ = esb("RZQ", [128, 2, 512], F32)
            ATT = esb("ATT", [128, 8, 512], BF16)
            OST = esb("OST", [128, 8, 512], F32)
            EPST = esb("EPST", [128, 2, D], F32)
            GAM = esb("GAM", [128, D], F32)
            BET = esb("BET", [128, D], F32)
            CRP = esb("CRP", [128, 1024], BF16)
            PQ = [eps_(f"pq{j}", [128, 512], F32) for j in range(2)]
            PZ = eps_("pz", [128, 512], F32)
            PA = eps_("pa", [128, 512], F32)
            PO = [eps_(f"po{j}", [128, 512], F32) for j in range(2)]
            PT32 = eps_("pt32", [128, 1024], F32)
            X, FM, WA, WB, TS, EPS, BOA, ONES, IDF = (self.X, self.FM, self.WA, self.WB,
                                                      self.TS, self.EPS, self.BOA,
                                                      self.ONES, self.IDF)
            P0pe, P0act, P0ve, P0ve2, P0pb, P0cc = (self.s_pe.n, self.s_act.n, self.s_ve.n,
                                                    self.s_ve2.n, self.s_pb.n, self.s_cc.n)
            P0ep = self.s_ep.n
            CR0 = P0ve2 + 4
            CR1 = P0ve2 + 8
            proj_pe = lambda h: P0pe + 58 * h
            grp_pe = lambda g: P0pe + 18 * (g // 2 + 1) + 20 * g
            proj_act = lambda h: P0act + 32 * h
            grp_act = lambda g: P0act + 16 * (g // 2 + 1) + 8 * g
            pv_ve = lambda h: P0ve + 52 * h
            grp_ve = lambda g: P0ve + 4 * (g // 2 + 1) + 24 * g
            grp_ve2 = lambda g: P0ve2 + 8 + 24 * g
            grp_pb = lambda g: P0pb + 4 * g

            with nc.Block() as block:
                def syn(s):
                    s.dma_start(out=GAM[:, :],
                                in_=self.gamma_v[None, :].broadcast_to([128, D])).then_inc(self.s_w.h, 16)
                    s.dma_start(out=BET[:, :],
                                in_=self.beta_v[None, :].broadcast_to([128, D])).then_inc(self.s_w.h, 16)
                    self.s_w.n += 32
                block.sync(syn)

                def gp(g):
                    for h in range(2):
                        g.wait_ge(self.s_cc.h, self.C_AG0 if h == 0 else self.C_AG1)
                        g.dma_start(out=self.QB[:, :, h * 512:(h + 1) * 512],
                                    in_=self.c_o[h][0]).then_inc(self.s_cc.h, 16)
                        self.s_cc.n += 16
                        for pc in range(4):
                            if 4 * h + pc >= 1:
                                g.wait_ge(self.s_ve2.h, P0ve2 + 4 * h + pc)
                            g.dma_start(out=CRP[:, 0:1024],
                                        in_=self.c_o[h][1][:, 2 * pc:2 * pc + 2, :]
                                        ).then_inc(self.s_cc.h, 16)
                            self.s_cc.n += 16
                block.gpsimd(gp)

                def ten(t):
                    t.wait_ge(self.s_w.h, self.WQ_A)
                    for hp2 in range(2):
                        ppe, pact = proj_pe(hp2), proj_act(hp2)
                        for gi in range(2):
                            for oc in range(8):
                                n = 8 * gi + oc
                                r = n % 2
                                t.wait_ge(self.s_act.h, pact + max(n - 1, 0))
                                for k in range(8):
                                    inst = t.matmul(PQ[r][:, :],
                                                    WA[:, k, oc * 128:(oc + 1) * 128],
                                                    FM[:, k, (2 * hp2 + gi) * 512:(2 * hp2 + gi + 1) * 512],
                                                    start=(k == 0), stop=(k == 7))
                                inst.then_inc(self.s_pe.h, 1)
                                self.s_pe.n += 1
                            t.wait_ge(self.s_act.h, pact + 8 * (gi + 1))
                            if hp2 + gi >= 1:
                                t.wait_ge(self.s_ve.h, pv_ve(hp2) + 2 * gi if gi else pv_ve(hp2 - 1) + 4)
                            for k in range(8):
                                inst = t.matmul(PZ[:, :], ONES[:, :],
                                                QEX[:, k, gi * 512:(gi + 1) * 512],
                                                start=(k == 0), stop=(k == 7))
                            inst.then_inc(self.s_pe.h, 1)
                            self.s_pe.n += 1
                        for gi in range(2):
                            g = 2 * hp2 + gi
                            gpe, gve, gpb = grp_pe(g), grp_ve(g), grp_pb(g)
                            if g == 0:
                                t.wait_ge(self.s_w.h, self.WO_A)
                            for ec in range(8):
                                if g == 0:
                                    t.wait_ge(self.s_ve2.h, CR0 if ec < 4 else CR1)
                                t.wait_ge(self.s_ve.h, gve + max(ec, 0))
                                for dc in range(8):
                                    inst = t.matmul(PA[:, :], CC[:, dc, ec * 128:(ec + 1) * 128],
                                                    QEX[:, dc, gi * 512:(gi + 1) * 512],
                                                    start=(dc == 0), stop=(dc == 7))
                                inst.then_inc(self.s_pe.h, 1)
                                self.s_pe.n += 1
                            for oc in range(8):
                                r = oc % 2
                                t.wait_ge(self.s_ve.h, gve + 8 + max(2 * (oc - 1), 0))
                                for ec in range(8):
                                    inst = t.matmul(PO[r][:, :], WB[:, ec, oc * 128:(oc + 1) * 128],
                                                    ATT[:, ec, :], start=(ec == 0), stop=(ec == 7))
                                inst.then_inc(self.s_pe.h, 1)
                                self.s_pe.n += 1
                            t.wait_ge(self.s_ve.h, gve + 24)
                            for sub in range(4):
                                t.wait_ge(self.s_pb.h, gpb + sub)
                                for oc in range(8):
                                    inst = t.matmul(PT32[:, oc * 128:(oc + 1) * 128],
                                                    OST[:, oc, sub * 128:(sub + 1) * 128], IDF[:, :],
                                                    start=True, stop=True)
                                inst.then_inc(self.s_pe.h, 1)
                                self.s_pe.n += 1
                block.tensor(ten)

                def sca(sc):
                    for hp2 in range(2):
                        ppe, pact = proj_pe(hp2), proj_act(hp2)
                        for n in range(16):
                            gi = n // 8
                            r = n % 2
                            sc.wait_ge(self.s_pe.h, ppe + n + 1 + gi)
                            sc.activation(QEX[:, n % 8, gi * 512:(gi + 1) * 512], PQ[r][:, :],
                                          AF.Exp).then_inc(self.s_act.h, 1)
                            self.s_act.n += 1
                        for gi in range(2):
                            g = 2 * hp2 + gi
                            gpe, gve2 = grp_pe(g), grp_ve2(g)
                            for sub in range(4):
                                i = g * 4 + sub
                                sc.wait_ge(self.s_ve2.h, gve2 + 6 * sub + 4)
                                sc.activation(CRP[:, :], EPST[:, 1, :], AF.Square,
                                              accum_out=TS[:, i, 1:2]).then_inc(self.s_act.h, 1)
                                self.s_act.n += 1
                                sc.wait_ge(self.s_act.h, self.s_act.n)
                                sc.activation(TS[:, i, 2:3], TS[:, i, 1:2], AF.Sqrt,
                                              bias=EPS[:, 0:1], scale=1.0 / D).then_inc(self.s_act.h, 1)
                                self.s_act.n += 1
                block.scalar(sca)

                def vec(v):
                    for h in range(2):
                        for pc in range(4):
                            v.wait_ge(self.s_cc.h, P0cc + 16 * (5 * h + pc + 2))
                            v.tensor_add(self.QB[:, 2 * pc:2 * pc + 2, h * 512:(h + 1) * 512],
                                         self.QB[:, 2 * pc:2 * pc + 2, h * 512:(h + 1) * 512],
                                         CRP[:, 0:1024].rearrange("p (c q) -> p c q", q=512)
                                         ).then_inc(self.s_ve2.h, 1)
                            self.s_ve2.n += 1
                    for hp2 in range(2):
                        ppe = proj_pe(hp2)
                        for gi in range(2):
                            v.wait_ge(self.s_pe.h, ppe + 9 * (gi + 1))
                            v.reciprocal(RZQ[:, gi, :], PZ[:, :]).then_inc(self.s_ve.h, 1)
                            self.s_ve.n += 1
                            v.wait_ge(self.s_ve.h, self.s_ve.n)
                            v.tensor_scalar(RZQ[:, gi, :], RZQ[:, gi, :], 1.0 / 32.0, None,
                                            ALU.mult).then_inc(self.s_ve.h, 1)
                            self.s_ve.n += 1
                        for gi in range(2):
                            g = 2 * hp2 + gi
                            gpe, gact = grp_pe(g), grp_act(g)
                            for ec in range(8):
                                v.wait_ge(self.s_pe.h, gpe + ec + 1)
                                v.tensor_copy(ATT[:, ec, :], PA[:, :]).then_inc(self.s_ve.h, 1)
                                self.s_ve.n += 1
                            for oc in range(8):
                                r = oc % 2
                                v.wait_ge(self.s_pe.h, gpe + 8 + oc + 1)
                                v.tensor_tensor(OST[:, oc, :], PO[r][:, :], RZQ[:, gi, :],
                                                ALU.mult).then_inc(self.s_ve.h, 1)
                                self.s_ve.n += 1
                                v.wait_ge(self.s_ve.h, self.s_ve.n)
                                v.tensor_scalar(OST[:, oc, :], OST[:, oc, :], BOA[:, oc:oc + 1], None,
                                                ALU.add).then_inc(self.s_ve.h, 1)
                                self.s_ve.n += 1
                            for sub in range(4):
                                i = g * 4 + sub
                                k = 4 * g + sub
                                if k >= 1:
                                    v.wait_ge(self.s_ep.h, P0ep + k)
                                v.wait_ge(self.s_pe.h, gpe + 16 + sub + 1)
                                u, w = EPST[:, 0, :], EPST[:, 1, :]
                                v.tensor_scalar(u, X[:, i, :], TS[:, i, 0:1], TS[:, i, 3:4],
                                                ALU.subtract, ALU.mult).then_inc(self.s_ve2.h, 1)
                                self.s_ve2.n += 1
                                v.wait_ge(self.s_ve2.h, self.s_ve2.n)
                                v.tensor_add(u, u, PT32[:, :]).then_inc(self.s_pb.h, 1)
                                self.s_pb.n += 1
                                v.wait_ge(self.s_pb.h, self.s_pb.n)
                                v.tensor_reduce(TS[:, i, 0:1], u, AX.X, ALU.add).then_inc(self.s_ve2.h, 1)
                                self.s_ve2.n += 1
                                v.wait_ge(self.s_ve2.h, self.s_ve2.n)
                                v.tensor_scalar(TS[:, i, 0:1], TS[:, i, 0:1], 1.0 / D, None,
                                                ALU.mult).then_inc(self.s_ve2.h, 1)
                                self.s_ve2.n += 1
                                v.wait_ge(self.s_ve2.h, self.s_ve2.n)
                                v.tensor_scalar(EPST[:, 1, :], u, TS[:, i, 0:1], None,
                                                ALU.subtract).then_inc(self.s_ve2.h, 1)
                                self.s_ve2.n += 1
                                v.wait_ge(self.s_act.h, grp_act(g) + 2 * (sub + 1))
                                v.reciprocal(TS[:, i, 3:4], TS[:, i, 2:3]).then_inc(self.s_ve2.h, 1)
                                self.s_ve2.n += 1
                                v.wait_ge(self.s_ve2.h, self.s_ve2.n)
                                v.scalar_tensor_tensor(EPST[:, 1, :], EPST[:, 1, :], TS[:, i, 3:4],
                                                       GAM[:, :], ALU.mult, ALU.mult
                                                       ).then_inc(self.s_ve2.h, 1)
                                self.s_ve2.n += 1
                block.vector(vec)

                def gp2(g):
                    for g_ in range(4):
                        gve2 = grp_ve2(g_)
                        for sub in range(4):
                            i = g_ * 4 + sub
                            k = 4 * g_ + sub
                            if k >= 1:
                                g.wait_ge(self.s_ep.h, self.s_ep.n)
                            g.wait_ge(self.s_ve2.h, gve2 + 6 * sub + 4)
                            g.tensor_add(EPST[:, 0, :], X[:, i, :], BET[:, :]).then_inc(self.s_gp.h, 1)
                            self.s_gp.n += 1
                            g.wait_ge(self.s_ve2.h, gve2 + 6 * sub + 6)
                            g.wait_ge(self.s_gp.h, self.s_gp.n)
                            g.tensor_add(X[:, i, :], EPST[:, 0, :], EPST[:, 1, :]
                                         ).then_inc(self.s_ep.h, 1)
                            self.s_ep.n += 1
                block.gpsimd(gp2)
            self.barrier()


    def ffn_phase(self):
        nc = self.nc
        self._ffn_ct = getattr(self, "_ffn_ct", 0) + 1
        u = f"{self._ffn_ct}"
        with ExitStack() as ctx:
            esb = lambda n, *a: ctx.enter_context(nc.sbuf_tensor(n + u, *a))
            eps_ = lambda n, *a: ctx.enter_context(nc.psum_tensor(n + u, *a))
            WIND = esb("WIND", [128, 8, 128], BF16)
            WOUTD = esb("WOUTD", [128, 8, 128], BF16)
            FCOL = esb("FCOL", [128, 48], F32)
            FMIS = esb("FMIS", [128, 2], F32)
            HH = esb("HH", [128, 2, 512], F32)
            GG = esb("GG", [128, 2, 512], F32)
            GGB = esb("GGB", [128, 512], BF16)
            OSTA = esb("OSTA", [128, 8, 512], F32)
            OSTB = esb("OSTB", [128, 8, 512], F32)
            SS = OSTA[:, 0, :]
            PSA = eps_("fpsa", [128, 512], F32)
            PSB_s = eps_("fpsb", [128, 512], F32)
            POA = eps_("fpoa", [128, 512], F32)
            POB = eps_("fpob", [128, 512], F32)
            PT32 = eps_("fpt32", [128, 1024], F32)
            X, FM, BOUT, IDF = self.X, self.FM, self.BOUT, self.IDF
            with nc.Block() as block:
                def syn(s):
                    s.dma_start(out=FCOL[:, :], in_=self.ffncols).then_inc(self.s_w.h, 16)
                    s.dma_start(out=FMIS[:, :], in_=self.ffnmisc).then_inc(self.s_w.h, 16)
                    self.s_w.n += 32
                block.sync(syn)

                def gp(g):
                    g.dma_start(out=WIND[:, :, :],
                                in_=self.windup.rearrange("(k p) j -> p k j", p=128)
                                ).then_inc(self.s_w.h, 16)
                    g.dma_start(out=WOUTD[:, :, :],
                                in_=self.woutdup.rearrange("p (c j) -> p c j", j=128)
                                ).then_inc(self.s_w.h, 16)
                    self.s_w.n += 32
                block.gpsimd(gp)
            FW = self.s_w.n

            p0, a0, v0, v20, pb0 = (self.s_pe.n, self.s_act.n, self.s_ve.n,
                                    self.s_ve2.n, self.s_pb.n)
            with nc.Block() as block:
                def ten(t):
                    t.wait_ge(self.s_w.h, FW)
                    for g2 in range(2):
                        peg, veg, v2g, pbg = (p0 + 18 * g2, v0 + 18 * g2,
                                              v20 + 17 * g2, pb0 + 8 * g2)
                        for k in range(8):
                            inst = t.matmul(PSA[0:64, :], WIND[:, k, 0:64],
                                            FM[:, k, g2 * 1024:g2 * 1024 + 512],
                                            start=(k == 0), stop=(k == 7), tile_position=(0, 0))
                        inst.then_inc(self.s_pe.h, 1)
                        self.s_pe.n += 1
                        for k in range(8):
                            inst = t.matmul(PSB_s[64:128, :], WIND[:, k, 64:128],
                                            FM[:, k, g2 * 1024 + 512:(g2 + 1) * 1024],
                                            start=(k == 0), stop=(k == 7), tile_position=(0, 64))
                        inst.then_inc(self.s_pe.h, 1)
                        self.s_pe.n += 1
                        t.wait_ge(self.s_ve2.h, v2g + 17)
                        for oc in range(8):
                            t.wait_ge(self.s_ve.h, veg + 2 + 2 * oc if oc >= 1 else veg + 2)
                            t.matmul(POA[:, :], WOUTD[0:64, oc, :], GGB[0:64, :],
                                     start=True, stop=True, tile_position=(0, 0))
                            inst = t.matmul(POB[:, :], WOUTD[64:128, oc, :], GGB[64:128, :],
                                            start=True, stop=True, tile_position=(64, 0))
                            inst.then_inc(self.s_pe.h, 1)
                            self.s_pe.n += 1
                        t.wait_ge(self.s_ve.h, veg + 18)
                        for half in range(2):
                            OST = OSTA if half == 0 else OSTB
                            for sub in range(4):
                                sb8 = half * 4 + sub
                                t.wait_ge(self.s_pb.h, pbg + sb8)
                                for oc in range(8):
                                    inst = t.matmul(PT32[:, oc * 128:(oc + 1) * 128],
                                                    OST[:, oc, sub * 128:(sub + 1) * 128],
                                                    IDF[:, :], start=True, stop=True)
                                inst.then_inc(self.s_pe.h, 1)
                                self.s_pe.n += 1
                block.tensor(ten)

                def sca(sc):
                    for g2 in range(2):
                        veg, v2g = v0 + 18 * g2, v20 + 17 * g2
                        sc.wait_ge(self.s_ve.h, veg + 2)
                        for j in range(16):
                            if j >= 2:
                                sc.wait_ge(self.s_ve2.h, v2g + (j - 1))
                            sc.activation(HH[:, j % 2, :], SS[:, :], AF.Relu,
                                          bias=FCOL[:, 16 + j:17 + j],
                                          scale=FCOL[:, j:j + 1]).then_inc(self.s_act.h, 1)
                            self.s_act.n += 1
                block.scalar(sca)

                def vec(v):
                    for g2 in range(2):
                        peg, ag, v2g = p0 + 18 * g2, a0 + 16 * g2, v20 + 17 * g2
                        v.wait_ge(self.s_pe.h, peg + 1)
                        if g2 >= 1:
                            v.wait_ge(self.s_act.h, a0 + 16)
                        v.tensor_scalar(SS[0:64, :], PSA[0:64, :], FMIS[0:64, 0:1], None,
                                        ALU.add).then_inc(self.s_ve.h, 1)
                        self.s_ve.n += 1
                        v.wait_ge(self.s_pe.h, peg + 2)
                        v.tensor_scalar(SS[64:128, :], PSB_s[64:128, :], FMIS[64:128, 0:1], None,
                                        ALU.add).then_inc(self.s_ve.h, 1)
                        self.s_ve.n += 1
                        for j in range(16):
                            v.wait_ge(self.s_act.h, ag + j + 1)
                            if j < 2:
                                v.tensor_scalar(GG[:, j, :], HH[:, j % 2, :],
                                                FCOL[:, 32 + j:33 + j], None,
                                                ALU.mult).then_inc(self.s_ve2.h, 1)
                            else:
                                if j >= 4:
                                    v.wait_ge(self.s_ve2.h, self.s_ve2.n - 1)
                                v.scalar_tensor_tensor(GG[:, j % 2, :], HH[:, j % 2, :],
                                                       FCOL[:, 32 + j:33 + j], GG[:, j % 2, :],
                                                       ALU.mult, ALU.add).then_inc(self.s_ve2.h, 1)
                            self.s_ve2.n += 1
                        v.wait_ge(self.s_ve2.h, self.s_ve2.n)
                        v.scalar_tensor_tensor(GGB[:, :], GG[:, 0, :], FMIS[:, 1:2], GG[:, 1, :],
                                               ALU.add, ALU.add).then_inc(self.s_ve2.h, 1)
                        self.s_ve2.n += 1
                        for oc in range(8):
                            v.wait_ge(self.s_pe.h, peg + 2 + oc + 1)
                            v.tensor_scalar(OSTA[:, oc, :], POA[:, :], BOUT[:, oc:oc + 1], None,
                                            ALU.add).then_inc(self.s_ve.h, 1)
                            self.s_ve.n += 1
                            v.tensor_scalar(OSTB[:, oc, :], POB[:, :], BOUT[:, oc:oc + 1], None,
                                            ALU.add).then_inc(self.s_ve.h, 1)
                            self.s_ve.n += 1
                        for half in range(2):
                            for sub in range(4):
                                sb8 = half * 4 + sub
                                i = g2 * 8 + half * 4 + sub
                                v.wait_ge(self.s_pe.h, peg + 10 + sb8 + 1)
                                v.tensor_add(self.X[:, i, :], self.X[:, i, :], PT32[:, :]
                                             ).then_inc(self.s_pb.h, 1)
                                self.s_pb.n += 1
                block.vector(vec)
            self.barrier()


    def phaseB_proj(self):
        nc = self.nc
        with ExitStack() as ctx:
            esb = lambda *a: ctx.enter_context(nc.sbuf_tensor(*a))
            eps_ = lambda *a: ctx.enter_context(nc.psum_tensor(*a))
            COS = esb("COS", [128, T], BF16)
            SIN = esb("SIN", [128, T], BF16)
            RK = esb("RK", [128, 2, T], BF16)
            RKS = esb("RKS", [128, 2, T], BF16)
            KBP = esb("KBP", [128, 2, T], BF16)
            VTP = esb("VTP", [128, 2, 2, 512], BF16)
            BVB = esb("BVBp", [128, D], F32)
            PERM = esb("PERMp", [128, 128], BF16)
            self.BVB = BVB
            PKB = [eps_(f"bpk{j}", [128, 512], F32) for j in range(2)]
            PKR = [eps_(f"bpr{j}", [128, 512], F32) for j in range(2)]
            FM, QB, WA, WB = self.FM, self.QB, self.WA, self.WB
            with nc.Block() as block:
                def syn(s):
                    s.dma_start(out=COS[:, :], in_=self.cos_i).then_inc(self.s_w.h, 16)
                    s.dma_start(out=SIN[:, :], in_=self.sin_i).then_inc(self.s_w.h, 16)
                    s.dma_start(out=BVB[:, :],
                                in_=self.bvb_v[None, :].broadcast_to([128, D])).then_inc(self.s_w.h, 16)
                    s.dma_start(out=PERM[:, :], in_=self.perm_i).then_inc(self.s_w.h, 16)
                    self.s_w.n += 64
                block.sync(syn)

                def gp(g):
                    g.dma_start(out=WA[:, :, :],
                                in_=self.wkt_b.rearrange("(k p) o -> p k o", p=128)
                                ).then_inc(self.s_w.h, 16)
                    g.dma_start(out=WB[:, :, :],
                                in_=self.wvt_b.rearrange("(k p) o -> p k o", p=128)
                                ).then_inc(self.s_w.h, 16)
                    self.s_w.n += 32
                block.gpsimd(gp)
            BW0 = self.s_w.n

            vp0, vv0, vkv0 = self.s_pe.n, self.s_ve.n, self.s_kv.n
            with nc.Block() as block:
                def ten(t):
                    t.wait_ge(self.s_w.h, BW0)
                    for i in range(NT):
                        for eh in range(2):
                            m = 2 * i + eh
                            r = m % 2
                            t.wait_ge(self.s_ve.h, vv0 + max(m - 1, 0))
                            for k in range(8):
                                inst = t.matmul(PKB[r][:, :], FM[:, k, i * 128:(i + 1) * 128],
                                                WB[:, k, eh * 512:(eh + 1) * 512],
                                                start=(k == 0), stop=(k == 7))
                            inst.then_inc(self.s_pe.h, 1)
                            self.s_pe.n += 1
                block.tensor(ten)

                def vec(v):
                    for i in range(NT):
                        for eh in range(2):
                            m = 2 * i + eh
                            r = m % 2
                            if eh == 0 and i >= 2:
                                v.wait_ge(self.s_kv.h, vkv0 + 16 * (i - 1))
                            v.wait_ge(self.s_pe.h, vp0 + m + 1)
                            v.tensor_add(VTP[:, i % 2, eh, :], PKB[r][:, :],
                                         self.BVB[:, eh * 512:(eh + 1) * 512]
                                         ).then_inc(self.s_ve.h, 1)
                            self.s_ve.n += 1
                block.vector(vec)

                def syn(s):
                    for i in range(NT):
                        s.wait_ge(self.s_ve.h, vv0 + 2 * (i + 1))
                        s.dma_start(
                            out=self.kv_d[1].rearrange("(t d) -> t d", t=T)[i * 128:(i + 1) * 128, :],
                            in_=VTP[:, i % 2, :, :].rearrange("p a b -> p (a b)")
                            ).then_inc(self.s_kv.h, 16)
                        self.s_kv.n += 16
                block.sync(syn)

            with nc.Block() as block:
                def gp(g, tgt=self.s_kv.n):
                    g.wait_ge(self.s_kv.h, tgt)
                    g.collective_compute("AllGather", ALU.bypass, replica_groups=GROUPS,
                                         ins=[self.kv_d[1]], outs=[self.kvV_o]).then_inc(self.s_cc.h, 1)
                    self.s_cc.n += 1
                    g.dma_start(out=WB[:, :, :],
                                in_=self.wot_b.rearrange("(k p) o -> p k o", p=128)
                                ).then_inc(self.s_w.h, 16)
                    self.s_w.n += 16
                block.gpsimd(gp)
            self.KV_AGV = self.s_cc.n
            self.WO_B = self.s_w.n

            def rope_path(which):
                BIA = self.BKB if which == "K" else self.BQB
                WGATE = BW0 if which == "K" else self.WQ_B
                rp0, rv0, rv20, rg0, rdk0 = (self.s_pe.n, self.s_ve.n, self.s_ve2.n,
                                             self.s_gp.n, self.s_dk.n)
                with nc.Block() as block:
                    def ten(t):
                        t.wait_ge(self.s_w.h, WGATE)
                        for c in range(8):
                            cb = c % 2
                            for tg in range(4):
                                m = 4 * c + tg
                                r = m % 2
                                if m >= 2:
                                    mm = m - 2
                                    t.wait_ge(self.s_ve.h, rv0 + 8 * (mm // 4) + (mm % 4) + 1)
                                for k in range(8):
                                    inst = t.matmul(PKB[r][:, :], WA[:, k, c * 128:(c + 1) * 128],
                                                    FM[:, k, tg * 512:(tg + 1) * 512],
                                                    start=(k == 0), stop=(k == 7))
                                inst.then_inc(self.s_pe.h, 1)
                                self.s_pe.n += 1
                            t.wait_ge(self.s_ve.h, rv0 + 8 * c + 4)
                            for q in range(4):
                                mq = 4 * c + q
                                r2 = mq % 2
                                if mq >= 2:
                                    mm = mq - 2
                                    t.wait_ge(self.s_ve.h, rv0 + 8 * (mm // 4) + (mm % 4) + 5)
                                inst = t.matmul(PKR[r2][:, :], PERM[:, :],
                                                RK[:, cb, q * 512:(q + 1) * 512],
                                                start=True, stop=True)
                                inst.then_inc(self.s_pe.h, 1)
                                self.s_pe.n += 1
                    block.tensor(ten)

                    def vec(v):
                        def front(c):
                            cb = c % 2
                            for tg in range(4):
                                r = (4 * c + tg) % 2
                                v.wait_ge(self.s_pe.h, rp0 + 8 * c + tg + 1)
                                v.tensor_scalar(RK[:, cb, tg * 512:(tg + 1) * 512], PKB[r][:, :],
                                                BIA[:, c:c + 1], None, ALU.add
                                                ).then_inc(self.s_ve.h, 1)
                                self.s_ve.n += 1
                            for q in range(4):
                                r2 = (4 * c + q) % 2
                                v.wait_ge(self.s_pe.h, rp0 + 8 * c + 4 + q + 1)
                                v.tensor_copy(RKS[:, cb, q * 512:(q + 1) * 512], PKR[r2][:, :]
                                              ).then_inc(self.s_ve.h, 1)
                                self.s_ve.n += 1
                            v.tensor_tensor(RKS[:, cb, :], RKS[:, cb, :], SIN[:, :], ALU.mult
                                            ).then_inc(self.s_ve2.h, 1)
                            self.s_ve2.n += 1

                        def tail(c):
                            cb = c % 2
                            v.wait_ge(self.s_gp.h, rg0 + c + 1)
                            if which == "K" and c >= 2:
                                v.wait_ge(self.s_dk.h, rdk0 + 16 * (c - 1))
                            dst = KBP[:, cb, :] if which == "K" else QB[:, c, :]
                            v.tensor_add(dst, RK[:, cb, :], RKS[:, cb, :]).then_inc(self.s_ve2.h, 1)
                            self.s_ve2.n += 1

                        for c in range(9):
                            if c < 8:
                                front(c)
                            if c >= 1:
                                tail(c - 1)
                    block.vector(vec)

                    def gp(g):
                        for c in range(8):
                            cb = c % 2
                            g.wait_ge(self.s_pe.h, rp0 + 8 * c + 8)
                            g.tensor_tensor(RK[:, cb, :], RK[:, cb, :], COS[:, :], ALU.mult
                                            ).then_inc(self.s_gp.h, 1)
                            self.s_gp.n += 1
                    block.gpsimd(gp)

                    if which == "K":
                        def syn(s):
                            for c in range(8):
                                cb = c % 2
                                s.wait_ge(self.s_ve2.h, rv20 + (2 * c + 3 if c < 7 else 16))
                                s.dma_start(
                                    out=self.kv_d[0].rearrange("(d t) -> d t", d=D)[c * 128:(c + 1) * 128, :],
                                    in_=KBP[:, cb, :]).then_inc(self.s_dk.h, 16)
                                self.s_dk.n += 16
                        block.sync(syn)

            rope_path("K")

            with nc.Block() as block:
                def gp(g, tgt=self.s_dk.n):
                    g.wait_ge(self.s_dk.h, tgt)
                    g.collective_compute("AllGather", ALU.bypass, replica_groups=GROUPS,
                                         ins=[self.kv_d[0]], outs=[self.kvK_o]).then_inc(self.s_cc.h, 1)
                    self.s_cc.n += 1
                    g.dma_start(out=WA[:, :, :],
                                in_=self.wqt_b.rearrange("(k p) o -> p k o", p=128)
                                ).then_inc(self.s_w.h, 16)
                    self.s_w.n += 16
                block.gpsimd(gp)
            self.KV_AGK = self.s_cc.n
            self.WQ_B = self.s_w.n

            rope_path("Q")


    def phaseB_attn(self):
        nc = self.nc
        with ExitStack() as ctx:
            esb = lambda *a: ctx.enter_context(nc.sbuf_tensor(*a))
            eps_ = lambda *a: ctx.enter_context(nc.psum_tensor(*a))
            KVK = esb("KVK", [128, 2, 2, T], BF16)
            KVV = esb("KVV", [128, 2, 2, 16, 128], BF16)
            PT = [esb(f"PTx{j}", [128, 1024], BF16) for j in range(3)]
            OSB = esb("OSB", [128, 512], F32)
            OSZ = esb("OSZ", [128, 512], F32)
            ST = [eps_(f"bst{j}", [128, 1024], F32) for j in range(3)]
            PV = eps_("bpv", [128, 512], F32)
            PZ = eps_("bpz", [128, 512], F32)
            QB, FM, ONES = self.QB, self.FM, self.ONES
            kK = self.kvK_o.rearrange("a (d t) -> a d t", d=D)
            kV = self.kvV_o.rearrange("a (t d) -> a t d", t=T)

            b_kv0, b_pe0, b_act0, b_ve0 = (self.s_kv.n, self.s_pe.n, self.s_act.n,
                                           self.s_ve.n)
            with nc.Block() as block:
                def syn(s):
                    s.wait_ge(self.s_cc.h, self.KV_AGK)
                    for hp in range(8):
                        hb = hp % 2
                        if hp >= 2:
                            s.wait_ge(self.s_ve.h, b_ve0 + 12 * (hp - 1))
                        if hp == 0:
                            s.wait_ge(self.s_cc.h, self.KV_AGV)
                        for kh in range(2):
                            s.dma_start(out=KVK[:, hb, kh, :],
                                        in_=kK[kh, hp * 128:(hp + 1) * 128, :]
                                        ).then_inc(self.s_kv.h, 16)
                            s.dma_start(out=KVV[:, hb, kh, :, :],
                                        in_=kV[kh, :, hp * 128:(hp + 1) * 128]
                                        .rearrange("(kb p) d -> p kb d", p=128)
                                        ).then_inc(self.s_kv.h, 16)
                            self.s_kv.n += 32
                block.sync(syn)

                def ten(t):
                    for hp in range(8):
                        hb = hp % 2
                        b_pe = b_pe0 + 256 * hp
                        b_act = b_act0 + 128 * hp
                        b_ve = b_ve0 + 12 * hp
                        t.wait_ge(self.s_kv.h, b_kv0 + 64 * (hp + 1))

                        def st_mm(ck, hb=hb, hp=hp, b_act=b_act):
                            qg, kb = ck // 32, ck % 32
                            kh, kbl = kb // 16, kb % 16
                            j = ck % 3
                            t.wait_ge(self.s_act.h, b_act + max(ck - 2, 0))
                            t.matmul(ST[j][:, 0:512],
                                     KVK[0:64, hb, kh, kbl * 128:(kbl + 1) * 128],
                                     QB[0:64, hp, qg * 512:(qg + 1) * 512],
                                     start=True, stop=True, tile_position=(0, 0))
                            inst = t.matmul(ST[j][:, 512:1024],
                                            KVK[64:128, hb, kh, kbl * 128:(kbl + 1) * 128],
                                            QB[64:128, hp, qg * 512:(qg + 1) * 512],
                                            start=True, stop=True, tile_position=(64, 0))
                            inst.then_inc(self.s_pe.h, 1)
                            self.s_pe.n += 1

                        def av_mm(ck, hb=hb, b_act=b_act, b_ve=b_ve):
                            qg, kb = ck // 32, ck % 32
                            kh, kbl = kb // 16, kb % 16
                            j = ck % 3
                            t.wait_ge(self.s_act.h, b_act + ck + 1)
                            if kb == 0:
                                t.wait_ge(self.s_ve.h, b_ve + 3 * qg)
                            st_, sp = (kb == 0), (kb == 31)
                            t.matmul(PV[0:64, :], KVV[:, hb, kh, kbl, 0:64], PT[j][:, 0:512],
                                     start=st_, stop=sp, tile_position=(0, 0))
                            t.matmul(PV[64:128, :], KVV[:, hb, kh, kbl, 64:128], PT[j][:, 512:1024],
                                     start=st_, stop=sp, tile_position=(0, 64))
                            t.matmul(PZ[0:64, :], ONES[:, 0:64], PT[j][:, 0:512],
                                     start=st_, stop=sp, tile_position=(0, 0))
                            inst = t.matmul(PZ[64:128, :], ONES[:, 64:128], PT[j][:, 512:1024],
                                            start=st_, stop=sp, tile_position=(0, 64))
                            inst.then_inc(self.s_pe.h, 1)
                            self.s_pe.n += 1

                        for ck in range(130):
                            if ck < 128:
                                st_mm(ck)
                            if ck >= 2:
                                av_mm(ck - 2)
                block.tensor(ten)

                def sca(sc):
                    for hp in range(8):
                        b_pe = b_pe0 + 256 * hp
                        for ck in range(128):
                            j = ck % 3
                            sc.wait_ge(self.s_pe.h, b_pe + (ck + 1 if ck < 3 else 2 * ck - 1))
                            sc.activation(PT[j][:, :], ST[j][:, :], AF.Exp,
                                          scale=0.125).then_inc(self.s_act.h, 1)
                            self.s_act.n += 1
                block.scalar(sca)

                def vec(v):
                    for hp in range(8):
                        b_pe = b_pe0 + 256 * hp
                        for qg in range(4):
                            v.wait_ge(self.s_pe.h, b_pe + min(2 * (32 * qg + 31) + 4, 256))
                            v.tensor_copy(OSB[:, :], PV[:, :]).then_inc(self.s_ve.h, 1)
                            v.reciprocal(OSZ[:, :], PZ[:, :]).then_inc(self.s_ve.h, 1)
                            self.s_ve.n += 2
                            v.wait_ge(self.s_ve.h, self.s_ve.n)
                            v.tensor_tensor(FM[:, hp, qg * 512:(qg + 1) * 512], OSB[:, :],
                                            OSZ[:, :], ALU.mult).then_inc(self.s_ve.h, 1)
                            self.s_ve.n += 1
                block.vector(vec)
            self.barrier()


        with ExitStack() as ctx:
            esb = lambda *a: ctx.enter_context(nc.sbuf_tensor(*a))
            eps_ = lambda *a: ctx.enter_context(nc.psum_tensor(*a))
            OST = esb("bOST", [128, 8, 512], F32)
            PO = [eps_(f"bpo{j}", [128, 512], F32) for j in range(2)]
            PT32 = eps_("bpt32", [128, 1024], F32)
            X, FM, WB, BOB, IDF = self.X, self.FM, self.WB, self.BOB, self.IDF
            for g in range(4):
                b_pe, b_ve, b_pb = self.s_pe.n, self.s_ve.n, self.s_pb.n
                with nc.Block() as block:
                    def ten(t, g=g, b_ve=b_ve, b_pb=b_pb):
                        for oc in range(8):
                            r = oc % 2
                            if oc >= 2:
                                t.wait_ge(self.s_ve.h, b_ve + oc - 1)
                            for hd in range(8):
                                inst = t.matmul(PO[r][:, :], WB[:, hd, oc * 128:(oc + 1) * 128],
                                                FM[:, hd, g * 512:(g + 1) * 512],
                                                start=(hd == 0), stop=(hd == 7))
                            inst.then_inc(self.s_pe.h, 1)
                            self.s_pe.n += 1
                        t.wait_ge(self.s_ve.h, b_ve + 8)
                        for sub in range(4):
                            if sub >= 1:
                                t.wait_ge(self.s_pb.h, b_pb + sub)
                            for oc in range(8):
                                inst = t.matmul(PT32[:, oc * 128:(oc + 1) * 128],
                                                OST[:, oc, sub * 128:(sub + 1) * 128],
                                                IDF[:, :], start=True, stop=True)
                            inst.then_inc(self.s_pe.h, 1)
                            self.s_pe.n += 1
                    block.tensor(ten)

                    def vec(v, g=g, b_pe=b_pe):
                        for oc in range(8):
                            r = oc % 2
                            v.wait_ge(self.s_pe.h, b_pe + oc + 1)
                            v.tensor_scalar(OST[:, oc, :], PO[r][:, :], BOB[:, oc:oc + 1], None,
                                            ALU.add).then_inc(self.s_ve.h, 1)
                            self.s_ve.n += 1
                        for sub in range(4):
                            i = g * 4 + sub
                            v.wait_ge(self.s_pe.h, b_pe + 8 + sub + 1)
                            v.tensor_add(X[:, i, :], X[:, i, :], PT32[:, :]).then_inc(self.s_pb.h, 1)
                            self.s_pb.n += 1
                    block.vector(vec)
            self.barrier()



    def p9_store(self):
        nc = self.nc
        with nc.Block() as block:
            def syn(s):
                xr = self.out_d.rearrange("(i p) d -> p i d", p=128)
                s.dma_start(out=xr, in_=self.X[:, :, :]).then_inc(self.s_st.h, 16)
                self.s_st.n += 16
                s.wait_ge(self.s_st.h, self.s_st.n)
            block.sync(syn)


def host_prep(inputs):
    x = np.asarray(inputs["L0_attn_a_x"])
    f32 = lambda a: np.ascontiguousarray(np.asarray(a), dtype=np.float32)
    bcol = lambda v: np.ascontiguousarray(np.asarray(v, np.float32).reshape(8, 128).T)

    com = {}
    com["wq_a"] = f32(inputs["L0_attn_a_Wq"])
    com["wk_a"] = f32(inputs["L0_attn_a_Wk"])
    com["wv_a"] = f32(inputs["L0_attn_a_Wv"])
    com["wo_a"] = f32(inputs["L0_attn_a_Wo"])
    com["boa_c"] = bcol(inputs["L0_attn_a_bo"])
    com["gamma_v"] = f32(inputs["L0_attn_a_gamma"])
    com["beta_v"] = f32(inputs["L0_attn_a_beta"])
    tb = lambda a: np.ascontiguousarray(np.asarray(a).T)
    com["wqt_b"] = tb(inputs["L0_attn_b_Wq"])
    com["wkt_b"] = tb(inputs["L0_attn_b_Wk"])
    com["wvt_b"] = tb(inputs["L0_attn_b_Wv"])
    com["wot_b"] = tb(inputs["L0_attn_b_Wo"])
    com["bqb_c"] = bcol(np.asarray(inputs["L0_attn_b_bq"], np.float32))
    com["bkb_c"] = bcol(np.asarray(inputs["L0_attn_b_bk"], np.float32))
    com["bvb_v"] = f32(inputs["L0_attn_b_bv"])
    com["bob_c"] = bcol(np.asarray(inputs["L0_attn_b_bo"], np.float32))
    win = f32(inputs["L0_ffn_Win"])
    com["windup"] = np.ascontiguousarray(np.concatenate([win, win], axis=1))
    wout = f32(inputs["L0_ffn_Wout"])
    com["woutdup"] = np.ascontiguousarray(np.concatenate([wout, wout], axis=0))
    w1 = f32(inputs["L0_ffn_W1"]); b1 = f32(inputs["L0_ffn_b1"]); w2 = f32(inputs["L0_ffn_W2"])
    dup = lambda a: np.concatenate([a, a], axis=0)
    com["ffncols"] = np.ascontiguousarray(
        np.concatenate([dup(w1), dup(b1), dup(w2)], axis=1))
    bin_ = f32(inputs["L0_ffn_bin_"]); b2 = f32(inputs["L0_ffn_b2"])
    com["ffnmisc"] = np.ascontiguousarray(
        np.stack([np.concatenate([bin_, bin_]), np.concatenate([b2, b2])], axis=1))
    com["boutc"] = bcol(inputs["L0_ffn_bout"])
    com["idb"] = np.eye(128, dtype=BF)
    com["idf"] = np.eye(128, dtype=np.float32)
    com["onesb"] = np.ones((128, 128), dtype=BF)
    perm = np.zeros((128, 128), dtype=BF)
    for p in range(128):
        src = (p // 64) * 64 + ((p % 64) + 32) % 64
        perm[src, p] = 1.0
    com["permb"] = perm
    com["epsv"] = np.array([1e-5, 0, 0, 0], np.float32)

    inv = 1.0 / (10000.0 ** (np.arange(0, 64, 2, dtype=np.float64) / 64.0))
    in_maps = []
    for c in range(8):
        b, h = c // 2, c % 2
        m = dict(com)
        m["x"] = np.ascontiguousarray(x[b, h * T:(h + 1) * T, :], dtype=np.float32)
        t = np.arange(h * T, (h + 1) * T, dtype=np.float64)
        ang = np.outer(inv, t)
        cos64 = np.concatenate([np.cos(ang), np.cos(ang)], axis=0)
        sin32 = np.sin(ang)
        sinS64 = np.concatenate([-sin32, sin32], axis=0)
        m["cosT"] = np.ascontiguousarray(np.concatenate([cos64, cos64], axis=0).astype(BF))
        m["sinS"] = np.ascontiguousarray(np.concatenate([sinS64, sinS64], axis=0).astype(BF))
        in_maps.append(m)
    return in_maps


_CACHE = {}


def get_nc(stages=4):
    if stages not in _CACHE:
        _CACHE[stages] = Builder(stages).build()
    return _CACHE[stages]


def kernel(**inputs):
    in_maps = host_prep(inputs)
    nc = get_nc(4)
    res = run_bass_kernel_spmd(nc, in_maps, core_ids=list(range(8)))
    x = np.asarray(inputs["L0_attn_a_x"])
    out = np.empty((4, 4096, 1024), dtype=np.float32)
    for c in range(8):
        b, h = c // 2, c % 2
        out[b, h * T:(h + 1) * T, :] = res.results[c]["out"]
    return out


# version 12 (final)
# speedup vs baseline: 1.6033x; 1.6033x over previous
import numpy as np
import ml_dtypes
from contextlib import ExitStack

import concourse.bass as bass
import concourse.mybir as mybir
from concourse.bass_utils import run_bass_kernel_spmd

BF = ml_dtypes.bfloat16
F32 = mybir.dt.float32
BF16 = mybir.dt.bfloat16
AF = mybir.ActivationFunctionType
ALU = mybir.AluOpType
AX = mybir.AxisListType

T = 2048
NT = 16
D = 1024
GROUPS = [[0, 1], [2, 3], [4, 5], [6, 7]]


class Sem:
    def __init__(self, h):
        self.h = h
        self.n = 0

    def inc(self, inst, d=1):
        inst.then_inc(self.h, d)
        self.n += d
        return self.n


class Builder:
    def __init__(self, stages=4):
        self.stages = stages
        nc = self.nc = bass.Bass("TRN2", target_bir_lowering=False)
        di = lambda n, s, dt: nc.dram_tensor(n, s, dt, kind="ExternalInput").ap()
        self.x_in = di("x", [T, D], F32)
        self.wq_a = di("wq_a", [D, D], F32)
        self.wk_a = di("wk_a", [D, D], F32)
        self.wv_a = di("wv_a", [D, D], F32)
        self.wo_a = di("wo_a", [D, D], F32)
        self.boa_c = di("boa_c", [128, 8], F32)
        self.gamma_v = di("gamma_v", [D], F32)
        self.beta_v = di("beta_v", [D], F32)
        self.wqt_b = di("wqt_b", [D, D], BF16)
        self.wkt_b = di("wkt_b", [D, D], BF16)
        self.wvt_b = di("wvt_b", [D, D], BF16)
        self.wot_b = di("wot_b", [D, D], BF16)
        self.bqb_c = di("bqb_c", [128, 8], F32)
        self.bkb_c = di("bkb_c", [128, 8], F32)
        self.bvb_v = di("bvb_v", [D], F32)
        self.bob_c = di("bob_c", [128, 8], F32)
        self.windup = di("windup", [D, 128], F32)
        self.woutdup = di("woutdup", [128, D], F32)
        self.ffncols = di("ffncols", [128, 48], F32)
        self.ffnmisc = di("ffnmisc", [128, 2], F32)
        self.boutc = di("boutc", [128, 8], F32)
        self.cos_i = di("cosT", [128, T], BF16)
        self.sin_i = di("sinS", [128, T], BF16)
        self.idb_i = di("idb", [128, 128], BF16)
        self.idf_i = di("idf", [128, 128], F32)
        self.ones_i = di("onesb", [128, 128], BF16)
        self.perm_i = di("permb", [128, 128], BF16)
        self.eps_i = di("epsv", [4], F32)
        self.out_d = nc.dram_tensor("out", [T, D], F32, kind="ExternalOutput").ap()

        self.kst_d = nc.dram_tensor("kst_d", [128, 16, 1024], BF16).ap()
        self.zk_d = nc.dram_tensor("zk_d", [128, 8], F32).ap()
        self.zk_o = nc.dram_tensor("zk_o", [2, 128, 8], F32).ap()
        self.c_d = nc.dram_tensor("c_d", [2, 128, 8, 512], BF16).ap()
        self.c_o = nc.dram_tensor("c_o", [2, 2, 128, 8, 512], BF16).ap()
        self.kv_d = nc.dram_tensor("kv_d", [2, T * D], BF16).ap()
        self.kvK_o = nc.dram_tensor("kvK_o", [2, T * D], BF16).ap()
        self.kvV_o = nc.dram_tensor("kvV_o", [2, T * D], BF16).ap()

    def barrier(self):
        tgt = self.s_bar.n + 5
        self.s_bar.n = tgt
        with self.nc.Block() as block:
            for name in ("sync", "tensor", "vector", "scalar", "gpsimd"):
                def body(e, tgt=tgt):
                    e.drain()
                    e.sem_inc(self.s_bar.h, 1)
                    e.wait_ge(self.s_bar.h, tgt)
                getattr(block, name)(body)

    def build(self):
        nc = self.nc
        with ExitStack() as top:
            sb = lambda *a: top.enter_context(nc.sbuf_tensor(*a))
            mksem = lambda n: Sem(top.enter_context(nc.semaphore(n)))
            self.X = sb("X", [128, NT, D], F32)
            self.FM = sb("FM", [128, 8, T], BF16)
            self.WA = sb("WA", [128, 8, 1024], BF16)
            self.WB = sb("WB", [128, 8, 1024], BF16)
            self.QB = sb("QB", [128, 8, T], BF16)
            self.BOA = sb("BOA", [128, 8], F32)
            self.BQB = sb("BQB", [128, 8], F32)
            self.BKB = sb("BKB", [128, 8], F32)
            self.BOB = sb("BOB", [128, 8], F32)
            self.BOUT = sb("BOUT", [128, 8], F32)
            self.IDB = sb("IDB", [128, 128], BF16)
            self.IDF = sb("IDF", [128, 128], F32)
            self.ONES = sb("ONES", [128, 128], BF16)
            self.EPS = sb("EPS", [128, 4], F32)
            self.TS = sb("TS", [128, NT, 4], F32)
            self.ZKS = sb("ZKS", [128, 8, 6], F32)
            self.ZKG = sb("ZKG", [128, 8, 2], F32)

            for s in ("s_x", "s_w", "s_pe", "s_act", "s_ve", "s_ve2", "s_gp",
                      "s_dk", "s_cc", "s_st", "s_bar", "s_pb", "s_ep", "s_kv"):
                setattr(self, s, mksem(s))

            self.p0_loads()
            self.ln_phase(first=True, keep_stats=True)
            self.barrier()
            st = self.stages
            if st >= 1:
                self.phaseA_K()
                self.barrier()
                self.phaseA_VC()
                self.phaseA_QAttnOut()
                self.barrier()
            if st == 15 or st >= 2:
                self.ln_phase(first=False, keep_stats=False)
                self.barrier()
            if st >= 2 and st != 15:
                self.ffn_phase()
                self.barrier()
            if st >= 3 and st != 15:
                self.ln_phase(first=False, keep_stats=False)
                self.barrier()
                self.phaseB_proj()
                self.barrier()
                self.phaseB_attn()
                self.barrier()
            if st >= 4 and st != 15:
                self.ln_phase(first=False, keep_stats=False)
                self.barrier()
                self.ffn_phase()
                self.barrier()
            self.p9_store()
        return nc

    def p0_loads(self):
        nc = self.nc
        with nc.Block() as block:
            def sync_body(s):
                b = lambda v, w: v[None, :].broadcast_to([128, w])
                s.dma_start(out=self.EPS[:, :], in_=b(self.eps_i, 4)).then_inc(self.s_w.h, 16)
                s.dma_start(out=self.BOA[:, :], in_=self.boa_c).then_inc(self.s_w.h, 16)
                s.dma_start(out=self.BQB[:, :], in_=self.bqb_c).then_inc(self.s_w.h, 16)
                s.dma_start(out=self.BKB[:, :], in_=self.bkb_c).then_inc(self.s_w.h, 16)
                s.dma_start(out=self.BOB[:, :], in_=self.bob_c).then_inc(self.s_w.h, 16)
                s.dma_start(out=self.BOUT[:, :], in_=self.boutc).then_inc(self.s_w.h, 16)
                s.dma_start(out=self.IDB[:, :], in_=self.idb_i).then_inc(self.s_w.h, 16)
                s.dma_start(out=self.IDF[:, :], in_=self.idf_i).then_inc(self.s_w.h, 16)
                s.dma_start(out=self.ONES[:, :], in_=self.ones_i).then_inc(self.s_w.h, 16)
                self.s_w.n += 9 * 16
                xr = self.x_in.rearrange("(i p) d -> p i d", p=128)
                for ch in range(4):
                    s.dma_start(out=self.X[:, ch * 4:(ch + 1) * 4, :],
                                in_=xr[:, ch * 4:(ch + 1) * 4, :]).then_inc(self.s_x.h, 16)
                self.s_x.n += 64
            block.sync(sync_body)

            def gp_body(g):
                g.dma_start(out=self.WA[:, :, :],
                            in_=self.wk_a.rearrange("(k p) o -> p k o", p=128)).then_inc(self.s_w.h, 16)
                g.dma_start(out=self.WB[:, :, :],
                            in_=self.wv_a.rearrange("(k p) o -> p k o", p=128)).then_inc(self.s_w.h, 16)
                self.s_w.n += 32
            block.gpsimd(gp_body)
        self.W0 = self.s_w.n

    def ln_phase(self, first, keep_stats):
        nc = self.nc
        self._ln_ct = getattr(self, "_ln_ct", 0) + 1
        u = f"{self._ln_ct}"
        with ExitStack() as ctx:
            PSB = [ctx.enter_context(nc.psum_tensor(f"lnp{u}_{j}", [128, 1024], BF16)) for j in range(2)]
            NTB = ctx.enter_context(nc.sbuf_tensor(f"lnNTB{u}", [128, 2, D], BF16))
            SQD = ctx.enter_context(nc.sbuf_tensor(f"lnSQD{u}", [128, D], BF16))
            b_ve, b_ve2, b_act, b_pe, b_pb = (self.s_ve.n, self.s_ve2.n, self.s_act.n,
                                              self.s_pe.n, self.s_pb.n)
            TS, X, EPS, FM, IDB = self.TS, self.X, self.EPS, self.FM, self.IDB
            with nc.Block() as block:
                def vec(v):
                    def front(i):
                        if first:
                            v.wait_ge(self.s_x.h, 16 * (i // 4 + 1))
                        r = i % 2
                        v.tensor_reduce(TS[:, i, 0:1], X[:, i, :], AX.X, ALU.add).then_inc(self.s_ve.h, 1)
                        self.s_ve.n += 1
                        v.wait_ge(self.s_ve.h, self.s_ve.n)
                        v.tensor_scalar(TS[:, i, 0:1], TS[:, i, 0:1], -1.0 / D, None,
                                        ALU.mult).then_inc(self.s_ve.h, 1)
                        self.s_ve.n += 1

                    def tail(i):
                        r = i % 2
                        v.wait_ge(self.s_act.h, b_act + 2 * (i + 1))
                        v.reciprocal(TS[:, i, 3:4], TS[:, i, 2:3]).then_inc(self.s_ve2.h, 1)
                        self.s_ve2.n += 1
                        v.wait_ge(self.s_ve2.h, self.s_ve2.n)
                        v.tensor_scalar(NTB[:, r, :], X[:, i, :], TS[:, i, 0:1], TS[:, i, 3:4],
                                        ALU.add, ALU.mult).then_inc(self.s_ve2.h, 1)
                        self.s_ve2.n += 1

                    def evac(i):
                        r = i % 2
                        v.wait_ge(self.s_pe.h, b_pe + (i + 1))
                        v.tensor_copy(FM[:, :, i * 128:(i + 1) * 128],
                                      PSB[r][:, :].rearrange("p (c j) -> p c j", j=128)
                                      ).then_inc(self.s_pb.h, 1)
                        self.s_pb.n += 1

                    for s in range(NT + 2):
                        if s < NT:
                            front(s)
                        if 1 <= s <= NT:
                            tail(s - 1)
                        if s >= 2:
                            evac(s - 2)
                block.vector(vec)

                def sca(sc):
                    sc.wait_ge(self.s_w.h, self.W0)
                    for i in range(NT):
                        sc.wait_ge(self.s_ve.h, b_ve + 2 * (i + 1))
                        sc.activation(SQD[:, :], X[:, i, :], AF.Square, scale=1.0,
                                      bias=TS[:, i, 0:1],
                                      accum_out=TS[:, i, 1:2]).then_inc(self.s_act.h, 1)
                        self.s_act.n += 1
                        sc.wait_ge(self.s_act.h, self.s_act.n)
                        sc.activation(TS[:, i, 2:3], TS[:, i, 1:2], AF.Sqrt,
                                      bias=EPS[:, 0:1], scale=1.0 / D).then_inc(self.s_act.h, 1)
                        self.s_act.n += 1
                block.scalar(sca)

                def ten(t):
                    for i in range(NT):
                        r = i % 2
                        t.wait_ge(self.s_ve2.h, b_ve2 + 2 * (i + 1))
                        if i >= 2:
                            t.wait_ge(self.s_pb.h, b_pb + (i - 1))
                        for j in range(8):
                            inst = t.transpose(PSB[r][:, j * 128:(j + 1) * 128],
                                               NTB[:, r, j * 128:(j + 1) * 128], IDB[:, :])
                        inst.then_inc(self.s_pe.h, 1)
                        self.s_pe.n += 1
                block.tensor(ten)
        self.barrier()

    def phaseA_K(self):
        nc = self.nc
        with ExitStack() as ctx:
            PK = [ctx.enter_context(nc.psum_tensor(f"pk{j}", [128, 512], F32)) for j in range(2)]
            PSB = [ctx.enter_context(nc.psum_tensor(f"pkb{j}", [128, 512], BF16)) for j in range(2)]
            KEXPP = ctx.enter_context(nc.sbuf_tensor("KEXPP", [128, 2, 512], BF16))
            KSTS = ctx.enter_context(nc.sbuf_tensor("KSTS", [128, 4, 1024], BF16))
            b_pe, b_act, b_ve, b_pb, b_dk = (self.s_pe.n, self.s_act.n, self.s_ve.n,
                                             self.s_pb.n, self.s_dk.n)
            WA, FM, ZKS, ZKG, IDB = self.WA, self.FM, self.ZKS, self.ZKG, self.IDB

            def pe_proj_idx(n):
                return b_pe + (1 if n == 0 else 2 * n)

            def pe_tr_idx(m):
                return b_pe + (2 * m + 3 if m < 31 else 64)

            with nc.Block() as block:
                def ten(t):
                    n = 0
                    for tg in range(4):
                        for oc in range(8):
                            r = n % 2
                            if n >= 2:
                                t.wait_ge(self.s_act.h, b_act + (n - 1))
                            for k in range(8):
                                inst = t.matmul(PK[r][:, :], WA[:, k, oc * 128:(oc + 1) * 128],
                                                FM[:, k, tg * 512:(tg + 1) * 512],
                                                start=(k == 0), stop=(k == 7))
                            inst.then_inc(self.s_pe.h, 1)
                            self.s_pe.n += 1
                            if n >= 1:
                                m = n - 1
                                rr = m % 2
                                t.wait_ge(self.s_act.h, b_act + (m + 1))
                                if m >= 2:
                                    t.wait_ge(self.s_pb.h, b_pb + (m - 1))
                                for sub in range(4):
                                    inst = t.transpose(PSB[rr][:, sub * 128:(sub + 1) * 128],
                                                       KEXPP[:, rr, sub * 128:(sub + 1) * 128],
                                                       IDB[:, :])
                                inst.then_inc(self.s_pe.h, 1)
                                self.s_pe.n += 1
                            n += 1
                    m = 31
                    rr = m % 2
                    t.wait_ge(self.s_act.h, b_act + 32)
                    t.wait_ge(self.s_pb.h, b_pb + 30)
                    for sub in range(4):
                        inst = t.transpose(PSB[rr][:, sub * 128:(sub + 1) * 128],
                                           KEXPP[:, rr, sub * 128:(sub + 1) * 128], IDB[:, :])
                    inst.then_inc(self.s_pe.h, 1)
                    self.s_pe.n += 1
                block.tensor(ten)

                def sca(sc):
                    n = 0
                    for tg in range(4):
                        for oc in range(8):
                            r = n % 2
                            sc.wait_ge(self.s_pe.h, pe_proj_idx(n))
                            sc.activation(KEXPP[:, r, :], PK[r][:, :], AF.Exp,
                                          accum_out=ZKS[:, oc, tg:tg + 1]).then_inc(self.s_act.h, 1)
                            self.s_act.n += 1
                            n += 1
                block.scalar(sca)

                def vec(v):
                    m = 0
                    for tg in range(4):
                        for oc in range(8):
                            rr = m % 2
                            v.wait_ge(self.s_pe.h, pe_tr_idx(m))
                            if tg >= 1:
                                v.wait_ge(self.s_dk.h, b_dk + 16 * tg)
                            v.tensor_copy(KSTS[:, :, oc * 128:(oc + 1) * 128],
                                          PSB[rr][:, :].rearrange("p (s j) -> p s j", j=128)
                                          ).then_inc(self.s_pb.h, 1)
                            self.s_pb.n += 1
                            m += 1
                    v.wait_ge(self.s_act.h, b_act + 32)
                    v.tensor_reduce(ZKS[:, :, 4:5], ZKS[:, :, 0:4], AX.X, ALU.add
                                    ).then_inc(self.s_ve.h, 1)
                    self.s_ve.n += 1
                block.vector(vec)

                def syn(s):
                    for tg in range(4):
                        s.wait_ge(self.s_pb.h, b_pb + 8 * (tg + 1))
                        s.dma_start(out=self.kst_d[:, tg * 4:(tg + 1) * 4, :],
                                    in_=KSTS[:, :, :]).then_inc(self.s_dk.h, 16)
                        self.s_dk.n += 16
                    s.wait_ge(self.s_ve.h, b_ve + 1)
                    s.dma_start(out=self.zk_d, in_=ZKS[:, :, 4],
                                allow_slow_non_contiguous=True).then_inc(self.s_cc.h, 16)
                    self.s_cc.n += 16
                block.sync(syn)

                def gp(g):
                    g.wait_ge(self.s_cc.h, self.s_cc.n)
                    g.collective_compute("AllGather", ALU.bypass, replica_groups=GROUPS,
                                         ins=[self.zk_d], outs=[self.zk_o]).then_inc(self.s_cc.h, 1)
                    self.s_cc.n += 1
                    g.wait_ge(self.s_cc.h, self.s_cc.n)
                    g.dma_start(out=ZKS[:, :, 4], in_=self.zk_o[0],
                                allow_slow_non_contiguous=True).then_inc(self.s_cc.h, 16)
                    g.dma_start(out=ZKS[:, :, 5], in_=self.zk_o[1],
                                allow_slow_non_contiguous=True).then_inc(self.s_cc.h, 16)
                    self.s_cc.n += 32
                    g.wait_ge(self.s_pe.h, b_pe + 62)
                    g.dma_start(out=self.WA[:, :, :],
                                in_=self.wq_a.rearrange("(k p) o -> p k o", p=128)
                                ).then_inc(self.s_w.h, 16)
                    self.s_w.n += 16
                block.gpsimd(gp)
            self.WQ_A = self.s_w.n
            zk_in = self.s_cc.n

            with nc.Block() as block:
                def vec2(v):
                    v.wait_ge(self.s_cc.h, zk_in)
                    v.tensor_add(ZKG[:, :, 0], ZKS[:, :, 4], ZKS[:, :, 5]).then_inc(self.s_ve.h, 1)
                    self.s_ve.n += 1
                    v.wait_ge(self.s_ve.h, self.s_ve.n)
                    v.reciprocal(ZKG[:, :, 1], ZKG[:, :, 0]).then_inc(self.s_ve.h, 1)
                    self.s_ve.n += 1
                block.vector(vec2)
            self.RZK_READY = self.s_ve.n

    def phaseA_VC(self):
        nc = self.nc
        with ExitStack() as ctx:
            VH = ctx.enter_context(nc.sbuf_tensor("VH", [128, NT, 512], BF16))
            KSTIN = ctx.enter_context(nc.sbuf_tensor("KSTIN", [128, 2, 2, 1024], BF16))
            PS = ctx.enter_context(nc.psum_tensor("avcps", [128, 8, 512], F32))
            FM, WB, ZKG = self.FM, self.WB, self.ZKG
            p0, v0, dk0 = self.s_pe.n, self.s_ve.n, self.s_dk.n
            with nc.Block() as block:
                def ten(t):
                    for h in range(2):
                        peh, veh, dkh = p0 + 32 * h, v0 + 24 * h, dk0 + 128 * h
                        for i in range(NT):
                            r = i % 2
                            t.wait_ge(self.s_ve.h, veh + max(i - 1, 0))
                            for k in range(8):
                                inst = t.matmul(PS[:, r, :], FM[:, k, i * 128:(i + 1) * 128],
                                                WB[:, k, h * 512:(h + 1) * 512],
                                                start=(k == 0), stop=(k == 7))
                            inst.then_inc(self.s_pe.h, 1)
                            self.s_pe.n += 1
                        t.wait_ge(self.s_ve.h, veh + 16)
                        for i in range(NT):
                            pc, sub = i // 2, i % 2
                            t.wait_ge(self.s_dk.h, dkh + 16 * (pc + 1))
                            for cch in range(8):
                                inst = t.matmul(PS[:, cch, :],
                                                KSTIN[:, pc % 2, sub, cch * 128:(cch + 1) * 128],
                                                VH[:, i, :],
                                                start=(i == 0), stop=(i == NT - 1))
                            inst.then_inc(self.s_pe.h, 1)
                            self.s_pe.n += 1
                block.tensor(ten)

                def vec(v):
                    for h in range(2):
                        peh, veh = p0 + 32 * h, v0 + 24 * h
                        for i in range(NT):
                            r = i % 2
                            v.wait_ge(self.s_pe.h, peh + i + 1)
                            v.tensor_copy(VH[:, i, :], PS[:, r, :]).then_inc(self.s_ve.h, 1)
                            self.s_ve.n += 1
                        v.wait_ge(self.s_pe.h, peh + 32)
                        v.wait_ge(self.s_ve.h, self.RZK_READY)
                        for cch in range(8):
                            v.tensor_scalar(self.QB[:, cch, h * 512:(h + 1) * 512],
                                            PS[:, cch, :], ZKG[:, cch, 1:2], None,
                                            ALU.mult).then_inc(self.s_ve.h, 1)
                            self.s_ve.n += 1
                block.vector(vec)

                def syn(s):
                    for h in range(2):
                        peh, dkh = p0 + 32 * h, dk0 + 128 * h
                        for pc in range(8):
                            if pc >= 2:
                                s.wait_ge(self.s_pe.h, peh + 16 + (pc - 1) * 2)
                            elif h == 1:
                                s.wait_ge(self.s_pe.h, p0 + 32)
                            s.dma_start(out=KSTIN[:, pc % 2, :, :],
                                        in_=self.kst_d[:, pc * 2:(pc + 1) * 2, :]
                                        ).then_inc(self.s_dk.h, 16)
                            self.s_dk.n += 16
                    for h in range(2):
                        s.wait_ge(self.s_ve.h, v0 + 24 * h + 24)
                        s.dma_start(out=self.c_d[h],
                                    in_=self.QB[:, :, h * 512:(h + 1) * 512]
                                    ).then_inc(self.s_cc.h, 16)
                        self.s_cc.n += 16
                block.sync(syn)

                ship_t = self.s_cc.n

                def gp(g):
                    g.wait_ge(self.s_cc.h, ship_t - 16)
                    g.collective_compute("AllGather", ALU.bypass, replica_groups=GROUPS,
                                         ins=[self.c_d[0]], outs=[self.c_o[0]]).then_inc(self.s_cc.h, 1)
                    self.s_cc.n += 1
                    g.wait_ge(self.s_cc.h, ship_t)
                    g.collective_compute("AllGather", ALU.bypass, replica_groups=GROUPS,
                                         ins=[self.c_d[1]], outs=[self.c_o[1]]).then_inc(self.s_cc.h, 1)
                    self.s_cc.n += 1
                    g.dma_start(out=self.WB[:, :, :],
                                in_=self.wo_a.rearrange("(k p) o -> p k o", p=128)
                                ).then_inc(self.s_w.h, 16)
                    self.s_w.n += 16
                block.gpsimd(gp)
            self.C_AG0 = ship_t + 1
            self.C_AG1 = ship_t + 2
            self.WO_A = self.s_w.n


    def phaseA_QAttnOut(self):
        nc = self.nc
        with ExitStack() as ctx:
            esb = lambda *a: ctx.enter_context(nc.sbuf_tensor(*a))
            eps_ = lambda *a: ctx.enter_context(nc.psum_tensor(*a))
            CC = self.QB[:, :, 0:1024]
            QEX = self.QB[:, :, 1024:2048]
            RZQ = esb("RZQ", [128, 2, 512], F32)
            ATT = esb("ATT", [128, 8, 512], BF16)
            OST = esb("OST", [128, 8, 512], F32)
            EPST = esb("EPST", [128, 2, D], F32)
            GAM = esb("GAM", [128, D], F32)
            BET = esb("BET", [128, D], F32)
            CRP = esb("CRP", [128, 1024], BF16)
            PQ = [eps_(f"pq{j}", [128, 512], F32) for j in range(2)]
            PZ = eps_("pz", [128, 512], F32)
            PA = eps_("pa", [128, 512], F32)
            PO = [eps_(f"po{j}", [128, 512], F32) for j in range(2)]
            PT32 = eps_("pt32", [128, 1024], F32)
            X, FM, WA, WB, TS, EPS, BOA, ONES, IDF = (self.X, self.FM, self.WA, self.WB,
                                                      self.TS, self.EPS, self.BOA,
                                                      self.ONES, self.IDF)
            P0pe, P0act, P0ve, P0ve2, P0pb, P0cc = (self.s_pe.n, self.s_act.n, self.s_ve.n,
                                                    self.s_ve2.n, self.s_pb.n, self.s_cc.n)
            P0ep = self.s_ep.n
            CR0 = P0ve2 + 4
            CR1 = P0ve2 + 8
            proj_pe = lambda h: P0pe + 58 * h
            grp_pe = lambda g: P0pe + 18 * (g // 2 + 1) + 20 * g
            proj_act = lambda h: P0act + 32 * h
            grp_act = lambda g: P0act + 16 * (g // 2 + 1) + 8 * g
            pv_ve = lambda h: P0ve + 52 * h
            grp_ve = lambda g: P0ve + 4 * (g // 2 + 1) + 24 * g
            grp_ve2 = lambda g: P0ve2 + 8 + 24 * g
            grp_pb = lambda g: P0pb + 4 * g

            with nc.Block() as block:
                def syn(s):
                    s.dma_start(out=GAM[:, :],
                                in_=self.gamma_v[None, :].broadcast_to([128, D])).then_inc(self.s_w.h, 16)
                    s.dma_start(out=BET[:, :],
                                in_=self.beta_v[None, :].broadcast_to([128, D])).then_inc(self.s_w.h, 16)
                    self.s_w.n += 32
                block.sync(syn)

                def gp(g):
                    for h in range(2):
                        g.wait_ge(self.s_cc.h, self.C_AG0 if h == 0 else self.C_AG1)
                        g.dma_start(out=self.QB[:, :, h * 512:(h + 1) * 512],
                                    in_=self.c_o[h][0]).then_inc(self.s_cc.h, 16)
                        self.s_cc.n += 16
                        for pc in range(4):
                            if 4 * h + pc >= 1:
                                g.wait_ge(self.s_ve2.h, P0ve2 + 4 * h + pc)
                            g.dma_start(out=CRP[:, 0:1024],
                                        in_=self.c_o[h][1][:, 2 * pc:2 * pc + 2, :]
                                        ).then_inc(self.s_cc.h, 16)
                            self.s_cc.n += 16
                block.gpsimd(gp)

                def ten(t):
                    t.wait_ge(self.s_w.h, self.WQ_A)
                    for hp2 in range(2):
                        ppe, pact = proj_pe(hp2), proj_act(hp2)
                        for gi in range(2):
                            for oc in range(8):
                                n = 8 * gi + oc
                                r = n % 2
                                t.wait_ge(self.s_act.h, pact + max(n - 1, 0))
                                for k in range(8):
                                    inst = t.matmul(PQ[r][:, :],
                                                    WA[:, k, oc * 128:(oc + 1) * 128],
                                                    FM[:, k, (2 * hp2 + gi) * 512:(2 * hp2 + gi + 1) * 512],
                                                    start=(k == 0), stop=(k == 7))
                                inst.then_inc(self.s_pe.h, 1)
                                self.s_pe.n += 1
                            t.wait_ge(self.s_act.h, pact + 8 * (gi + 1))
                            if hp2 + gi >= 1:
                                t.wait_ge(self.s_ve.h, pv_ve(hp2) + 2 * gi if gi else pv_ve(hp2 - 1) + 4)
                            for k in range(8):
                                inst = t.matmul(PZ[:, :], ONES[:, :],
                                                QEX[:, k, gi * 512:(gi + 1) * 512],
                                                start=(k == 0), stop=(k == 7))
                            inst.then_inc(self.s_pe.h, 1)
                            self.s_pe.n += 1
                        for gi in range(2):
                            g = 2 * hp2 + gi
                            gpe, gve, gpb = grp_pe(g), grp_ve(g), grp_pb(g)
                            if g == 0:
                                t.wait_ge(self.s_w.h, self.WO_A)
                            for ec in range(8):
                                if g == 0:
                                    t.wait_ge(self.s_ve2.h, CR0 if ec < 4 else CR1)
                                t.wait_ge(self.s_ve.h, gve + max(ec, 0))
                                for dc in range(8):
                                    inst = t.matmul(PA[:, :], CC[:, dc, ec * 128:(ec + 1) * 128],
                                                    QEX[:, dc, gi * 512:(gi + 1) * 512],
                                                    start=(dc == 0), stop=(dc == 7))
                                inst.then_inc(self.s_pe.h, 1)
                                self.s_pe.n += 1
                            for oc in range(8):
                                r = oc % 2
                                t.wait_ge(self.s_ve.h, gve + 8 + max(2 * (oc - 1), 0))
                                for ec in range(8):
                                    inst = t.matmul(PO[r][:, :], WB[:, ec, oc * 128:(oc + 1) * 128],
                                                    ATT[:, ec, :], start=(ec == 0), stop=(ec == 7))
                                inst.then_inc(self.s_pe.h, 1)
                                self.s_pe.n += 1
                            t.wait_ge(self.s_ve.h, gve + 24)
                            for sub in range(4):
                                t.wait_ge(self.s_pb.h, gpb + sub)
                                for oc in range(8):
                                    inst = t.matmul(PT32[:, oc * 128:(oc + 1) * 128],
                                                    OST[:, oc, sub * 128:(sub + 1) * 128], IDF[:, :],
                                                    start=True, stop=True)
                                inst.then_inc(self.s_pe.h, 1)
                                self.s_pe.n += 1
                block.tensor(ten)

                def sca(sc):
                    for hp2 in range(2):
                        ppe, pact = proj_pe(hp2), proj_act(hp2)
                        for n in range(16):
                            gi = n // 8
                            r = n % 2
                            sc.wait_ge(self.s_pe.h, ppe + n + 1 + gi)
                            sc.activation(QEX[:, n % 8, gi * 512:(gi + 1) * 512], PQ[r][:, :],
                                          AF.Exp).then_inc(self.s_act.h, 1)
                            self.s_act.n += 1
                        for gi in range(2):
                            g = 2 * hp2 + gi
                            gpe, gve2 = grp_pe(g), grp_ve2(g)
                            for sub in range(4):
                                i = g * 4 + sub
                                sc.wait_ge(self.s_ve2.h, gve2 + 6 * sub + 4)
                                sc.activation(CRP[:, :], EPST[:, 1, :], AF.Square,
                                              accum_out=TS[:, i, 1:2]).then_inc(self.s_act.h, 1)
                                self.s_act.n += 1
                                sc.wait_ge(self.s_act.h, self.s_act.n)
                                sc.activation(TS[:, i, 2:3], TS[:, i, 1:2], AF.Sqrt,
                                              bias=EPS[:, 0:1], scale=1.0 / D).then_inc(self.s_act.h, 1)
                                self.s_act.n += 1
                block.scalar(sca)

                def vec(v):
                    for h in range(2):
                        for pc in range(4):
                            v.wait_ge(self.s_cc.h, P0cc + 16 * (5 * h + pc + 2))
                            v.tensor_add(self.QB[:, 2 * pc:2 * pc + 2, h * 512:(h + 1) * 512],
                                         self.QB[:, 2 * pc:2 * pc + 2, h * 512:(h + 1) * 512],
                                         CRP[:, 0:1024].rearrange("p (c q) -> p c q", q=512)
                                         ).then_inc(self.s_ve2.h, 1)
                            self.s_ve2.n += 1
                    for hp2 in range(2):
                        ppe = proj_pe(hp2)
                        for gi in range(2):
                            v.wait_ge(self.s_pe.h, ppe + 9 * (gi + 1))
                            v.reciprocal(RZQ[:, gi, :], PZ[:, :]).then_inc(self.s_ve.h, 1)
                            self.s_ve.n += 1
                            v.wait_ge(self.s_ve.h, self.s_ve.n)
                            v.tensor_scalar(RZQ[:, gi, :], RZQ[:, gi, :], 1.0 / 32.0, None,
                                            ALU.mult).then_inc(self.s_ve.h, 1)
                            self.s_ve.n += 1
                        for gi in range(2):
                            g = 2 * hp2 + gi
                            gpe, gact = grp_pe(g), grp_act(g)
                            for ec in range(8):
                                v.wait_ge(self.s_pe.h, gpe + ec + 1)
                                v.tensor_copy(ATT[:, ec, :], PA[:, :]).then_inc(self.s_ve.h, 1)
                                self.s_ve.n += 1
                            for oc in range(8):
                                r = oc % 2
                                v.wait_ge(self.s_pe.h, gpe + 8 + oc + 1)
                                v.tensor_tensor(OST[:, oc, :], PO[r][:, :], RZQ[:, gi, :],
                                                ALU.mult).then_inc(self.s_ve.h, 1)
                                self.s_ve.n += 1
                                v.wait_ge(self.s_ve.h, self.s_ve.n)
                                v.tensor_scalar(OST[:, oc, :], OST[:, oc, :], BOA[:, oc:oc + 1], None,
                                                ALU.add).then_inc(self.s_ve.h, 1)
                                self.s_ve.n += 1
                            for sub in range(4):
                                i = g * 4 + sub
                                k = 4 * g + sub
                                if k >= 1:
                                    v.wait_ge(self.s_ep.h, P0ep + k)
                                v.wait_ge(self.s_pe.h, gpe + 16 + sub + 1)
                                u, w = EPST[:, 0, :], EPST[:, 1, :]
                                v.tensor_scalar(u, X[:, i, :], TS[:, i, 0:1], TS[:, i, 3:4],
                                                ALU.add, ALU.mult).then_inc(self.s_ve2.h, 1)
                                self.s_ve2.n += 1
                                v.wait_ge(self.s_ve2.h, self.s_ve2.n)
                                v.tensor_add(u, u, PT32[:, :]).then_inc(self.s_pb.h, 1)
                                self.s_pb.n += 1
                                v.wait_ge(self.s_pb.h, self.s_pb.n)
                                v.tensor_reduce(TS[:, i, 0:1], u, AX.X, ALU.add).then_inc(self.s_ve2.h, 1)
                                self.s_ve2.n += 1
                                v.wait_ge(self.s_ve2.h, self.s_ve2.n)
                                v.tensor_scalar(TS[:, i, 0:1], TS[:, i, 0:1], 1.0 / D, None,
                                                ALU.mult).then_inc(self.s_ve2.h, 1)
                                self.s_ve2.n += 1
                                v.wait_ge(self.s_ve2.h, self.s_ve2.n)
                                v.tensor_scalar(EPST[:, 1, :], u, TS[:, i, 0:1], None,
                                                ALU.subtract).then_inc(self.s_ve2.h, 1)
                                self.s_ve2.n += 1
                                v.wait_ge(self.s_act.h, grp_act(g) + 2 * (sub + 1))
                                v.reciprocal(TS[:, i, 3:4], TS[:, i, 2:3]).then_inc(self.s_ve2.h, 1)
                                self.s_ve2.n += 1
                                v.wait_ge(self.s_ve2.h, self.s_ve2.n)
                                v.scalar_tensor_tensor(EPST[:, 1, :], EPST[:, 1, :], TS[:, i, 3:4],
                                                       GAM[:, :], ALU.mult, ALU.mult
                                                       ).then_inc(self.s_ve2.h, 1)
                                self.s_ve2.n += 1
                block.vector(vec)

                def gp2(g):
                    for g_ in range(4):
                        gve2 = grp_ve2(g_)
                        for sub in range(4):
                            i = g_ * 4 + sub
                            k = 4 * g_ + sub
                            if k >= 1:
                                g.wait_ge(self.s_ep.h, self.s_ep.n)
                            g.wait_ge(self.s_ve2.h, gve2 + 6 * sub + 4)
                            g.tensor_add(EPST[:, 0, :], X[:, i, :], BET[:, :]).then_inc(self.s_gp.h, 1)
                            self.s_gp.n += 1
                            g.wait_ge(self.s_ve2.h, gve2 + 6 * sub + 6)
                            g.wait_ge(self.s_gp.h, self.s_gp.n)
                            g.tensor_add(X[:, i, :], EPST[:, 0, :], EPST[:, 1, :]
                                         ).then_inc(self.s_ep.h, 1)
                            self.s_ep.n += 1
                block.gpsimd(gp2)
            self.barrier()


    def ffn_phase(self):
        nc = self.nc
        self._ffn_ct = getattr(self, "_ffn_ct", 0) + 1
        u = f"{self._ffn_ct}"
        with ExitStack() as ctx:
            esb = lambda n, *a: ctx.enter_context(nc.sbuf_tensor(n + u, *a))
            eps_ = lambda n, *a: ctx.enter_context(nc.psum_tensor(n + u, *a))
            WIND = esb("WIND", [128, 8, 128], BF16)
            WOUTD = esb("WOUTD", [128, 8, 128], BF16)
            FCOL = esb("FCOL", [128, 48], F32)
            FMIS = esb("FMIS", [128, 2], F32)
            HH = esb("HH", [128, 2, 512], F32)
            GG = esb("GG", [128, 2, 512], F32)
            GGB = esb("GGB", [128, 512], BF16)
            OSTA = esb("OSTA", [128, 8, 512], F32)
            OSTB = esb("OSTB", [128, 8, 512], F32)
            SS = OSTA[:, 0, :]
            PSA = eps_("fpsa", [128, 512], F32)
            PSB_s = eps_("fpsb", [128, 512], F32)
            POA = eps_("fpoa", [128, 512], F32)
            POB = eps_("fpob", [128, 512], F32)
            PT32 = eps_("fpt32", [128, 1024], F32)
            X, FM, BOUT, IDF = self.X, self.FM, self.BOUT, self.IDF
            with nc.Block() as block:
                def syn(s):
                    s.dma_start(out=FCOL[:, :], in_=self.ffncols).then_inc(self.s_w.h, 16)
                    s.dma_start(out=FMIS[:, :], in_=self.ffnmisc).then_inc(self.s_w.h, 16)
                    self.s_w.n += 32
                block.sync(syn)

                def gp(g):
                    g.dma_start(out=WIND[:, :, :],
                                in_=self.windup.rearrange("(k p) j -> p k j", p=128)
                                ).then_inc(self.s_w.h, 16)
                    g.dma_start(out=WOUTD[:, :, :],
                                in_=self.woutdup.rearrange("p (c j) -> p c j", j=128)
                                ).then_inc(self.s_w.h, 16)
                    self.s_w.n += 32
                block.gpsimd(gp)
            FW = self.s_w.n

            p0, a0, v0, v20, pb0 = (self.s_pe.n, self.s_act.n, self.s_ve.n,
                                    self.s_ve2.n, self.s_pb.n)
            with nc.Block() as block:
                def ten(t):
                    t.wait_ge(self.s_w.h, FW)
                    for g2 in range(2):
                        peg, veg, v2g, pbg = (p0 + 18 * g2, v0 + 18 * g2,
                                              v20 + 17 * g2, pb0 + 8 * g2)
                        for k in range(8):
                            inst = t.matmul(PSA[0:64, :], WIND[:, k, 0:64],
                                            FM[:, k, g2 * 1024:g2 * 1024 + 512],
                                            start=(k == 0), stop=(k == 7), tile_position=(0, 0))
                        inst.then_inc(self.s_pe.h, 1)
                        self.s_pe.n += 1
                        for k in range(8):
                            inst = t.matmul(PSB_s[64:128, :], WIND[:, k, 64:128],
                                            FM[:, k, g2 * 1024 + 512:(g2 + 1) * 1024],
                                            start=(k == 0), stop=(k == 7), tile_position=(0, 64))
                        inst.then_inc(self.s_pe.h, 1)
                        self.s_pe.n += 1
                        t.wait_ge(self.s_ve2.h, v2g + 17)
                        for oc in range(8):
                            t.wait_ge(self.s_ve.h, veg + 2 + 2 * oc if oc >= 1 else veg + 2)
                            t.matmul(POA[:, :], WOUTD[0:64, oc, :], GGB[0:64, :],
                                     start=True, stop=True, tile_position=(0, 0))
                            inst = t.matmul(POB[:, :], WOUTD[64:128, oc, :], GGB[64:128, :],
                                            start=True, stop=True, tile_position=(64, 0))
                            inst.then_inc(self.s_pe.h, 1)
                            self.s_pe.n += 1
                        t.wait_ge(self.s_ve.h, veg + 18)
                        for half in range(2):
                            OST = OSTA if half == 0 else OSTB
                            for sub in range(4):
                                sb8 = half * 4 + sub
                                t.wait_ge(self.s_pb.h, pbg + sb8)
                                for oc in range(8):
                                    inst = t.matmul(PT32[:, oc * 128:(oc + 1) * 128],
                                                    OST[:, oc, sub * 128:(sub + 1) * 128],
                                                    IDF[:, :], start=True, stop=True)
                                inst.then_inc(self.s_pe.h, 1)
                                self.s_pe.n += 1
                block.tensor(ten)

                def sca(sc):
                    for g2 in range(2):
                        veg, v2g = v0 + 18 * g2, v20 + 17 * g2
                        sc.wait_ge(self.s_ve.h, veg + 2)
                        for j in range(16):
                            if j >= 2:
                                sc.wait_ge(self.s_ve2.h, v2g + (j - 1))
                            sc.activation(HH[:, j % 2, :], SS[:, :], AF.Relu,
                                          bias=FCOL[:, 16 + j:17 + j],
                                          scale=FCOL[:, j:j + 1]).then_inc(self.s_act.h, 1)
                            self.s_act.n += 1
                block.scalar(sca)

                def vec(v):
                    for g2 in range(2):
                        peg, ag, v2g = p0 + 18 * g2, a0 + 16 * g2, v20 + 17 * g2
                        v.wait_ge(self.s_pe.h, peg + 1)
                        if g2 >= 1:
                            v.wait_ge(self.s_act.h, a0 + 16)
                        v.tensor_scalar(SS[0:64, :], PSA[0:64, :], FMIS[0:64, 0:1], None,
                                        ALU.add).then_inc(self.s_ve.h, 1)
                        self.s_ve.n += 1
                        v.wait_ge(self.s_pe.h, peg + 2)
                        v.tensor_scalar(SS[64:128, :], PSB_s[64:128, :], FMIS[64:128, 0:1], None,
                                        ALU.add).then_inc(self.s_ve.h, 1)
                        self.s_ve.n += 1
                        for j in range(16):
                            v.wait_ge(self.s_act.h, ag + j + 1)
                            if j < 2:
                                v.tensor_scalar(GG[:, j, :], HH[:, j % 2, :],
                                                FCOL[:, 32 + j:33 + j], None,
                                                ALU.mult).then_inc(self.s_ve2.h, 1)
                            else:
                                if j >= 4:
                                    v.wait_ge(self.s_ve2.h, self.s_ve2.n - 1)
                                v.scalar_tensor_tensor(GG[:, j % 2, :], HH[:, j % 2, :],
                                                       FCOL[:, 32 + j:33 + j], GG[:, j % 2, :],
                                                       ALU.mult, ALU.add).then_inc(self.s_ve2.h, 1)
                            self.s_ve2.n += 1
                        v.wait_ge(self.s_ve2.h, self.s_ve2.n)
                        v.scalar_tensor_tensor(GGB[:, :], GG[:, 0, :], FMIS[:, 1:2], GG[:, 1, :],
                                               ALU.add, ALU.add).then_inc(self.s_ve2.h, 1)
                        self.s_ve2.n += 1
                        for oc in range(8):
                            v.wait_ge(self.s_pe.h, peg + 2 + oc + 1)
                            v.tensor_scalar(OSTA[:, oc, :], POA[:, :], BOUT[:, oc:oc + 1], None,
                                            ALU.add).then_inc(self.s_ve.h, 1)
                            self.s_ve.n += 1
                            v.tensor_scalar(OSTB[:, oc, :], POB[:, :], BOUT[:, oc:oc + 1], None,
                                            ALU.add).then_inc(self.s_ve.h, 1)
                            self.s_ve.n += 1
                        for half in range(2):
                            for sub in range(4):
                                sb8 = half * 4 + sub
                                i = g2 * 8 + half * 4 + sub
                                v.wait_ge(self.s_pe.h, peg + 10 + sb8 + 1)
                                v.tensor_add(self.X[:, i, :], self.X[:, i, :], PT32[:, :]
                                             ).then_inc(self.s_pb.h, 1)
                                self.s_pb.n += 1
                block.vector(vec)
            self.barrier()


    def phaseB_proj(self):
        nc = self.nc
        with ExitStack() as ctx:
            esb = lambda *a: ctx.enter_context(nc.sbuf_tensor(*a))
            eps_ = lambda *a: ctx.enter_context(nc.psum_tensor(*a))
            COS = esb("COS", [128, T], BF16)
            SIN = esb("SIN", [128, T], BF16)
            RK = esb("RK", [128, 2, T], BF16)
            RKS = esb("RKS", [128, 2, T], BF16)
            KBP = esb("KBP", [128, 2, T], BF16)
            VTP = esb("VTP", [128, 2, 2, 512], BF16)
            BVB = esb("BVBp", [128, D], F32)
            PERM = esb("PERMp", [128, 128], BF16)
            self.BVB = BVB
            PKB = [eps_(f"bpk{j}", [128, 512], F32) for j in range(2)]
            PKR = [eps_(f"bpr{j}", [128, 512], F32) for j in range(2)]
            FM, QB, WA, WB = self.FM, self.QB, self.WA, self.WB
            with nc.Block() as block:
                def syn(s):
                    s.dma_start(out=COS[:, :], in_=self.cos_i).then_inc(self.s_w.h, 16)
                    s.dma_start(out=SIN[:, :], in_=self.sin_i).then_inc(self.s_w.h, 16)
                    s.dma_start(out=BVB[:, :],
                                in_=self.bvb_v[None, :].broadcast_to([128, D])).then_inc(self.s_w.h, 16)
                    s.dma_start(out=PERM[:, :], in_=self.perm_i).then_inc(self.s_w.h, 16)
                    self.s_w.n += 64
                block.sync(syn)

                def gp(g):
                    g.dma_start(out=WA[:, :, :],
                                in_=self.wkt_b.rearrange("(k p) o -> p k o", p=128)
                                ).then_inc(self.s_w.h, 16)
                    g.dma_start(out=WB[:, :, :],
                                in_=self.wvt_b.rearrange("(k p) o -> p k o", p=128)
                                ).then_inc(self.s_w.h, 16)
                    self.s_w.n += 32
                block.gpsimd(gp)
            BW0 = self.s_w.n

            vp0, vv0, vkv0 = self.s_pe.n, self.s_ve.n, self.s_kv.n
            with nc.Block() as block:
                def ten(t):
                    t.wait_ge(self.s_w.h, BW0)
                    for i in range(NT):
                        for eh in range(2):
                            m = 2 * i + eh
                            r = m % 2
                            t.wait_ge(self.s_ve.h, vv0 + max(m - 1, 0))
                            for k in range(8):
                                inst = t.matmul(PKB[r][:, :], FM[:, k, i * 128:(i + 1) * 128],
                                                WB[:, k, eh * 512:(eh + 1) * 512],
                                                start=(k == 0), stop=(k == 7))
                            inst.then_inc(self.s_pe.h, 1)
                            self.s_pe.n += 1
                block.tensor(ten)

                def vec(v):
                    for i in range(NT):
                        for eh in range(2):
                            m = 2 * i + eh
                            r = m % 2
                            if eh == 0 and i >= 2:
                                v.wait_ge(self.s_kv.h, vkv0 + 16 * (i - 1))
                            v.wait_ge(self.s_pe.h, vp0 + m + 1)
                            v.tensor_add(VTP[:, i % 2, eh, :], PKB[r][:, :],
                                         self.BVB[:, eh * 512:(eh + 1) * 512]
                                         ).then_inc(self.s_ve.h, 1)
                            self.s_ve.n += 1
                block.vector(vec)

                def syn(s):
                    for i in range(NT):
                        s.wait_ge(self.s_ve.h, vv0 + 2 * (i + 1))
                        s.dma_start(
                            out=self.kv_d[1].rearrange("(t d) -> t d", t=T)[i * 128:(i + 1) * 128, :],
                            in_=VTP[:, i % 2, :, :].rearrange("p a b -> p (a b)")
                            ).then_inc(self.s_kv.h, 16)
                        self.s_kv.n += 16
                block.sync(syn)

            with nc.Block() as block:
                def gp(g, tgt=self.s_kv.n):
                    g.wait_ge(self.s_kv.h, tgt)
                    g.collective_compute("AllGather", ALU.bypass, replica_groups=GROUPS,
                                         ins=[self.kv_d[1]], outs=[self.kvV_o]).then_inc(self.s_cc.h, 1)
                    self.s_cc.n += 1
                    g.dma_start(out=WB[:, :, :],
                                in_=self.wot_b.rearrange("(k p) o -> p k o", p=128)
                                ).then_inc(self.s_w.h, 16)
                    self.s_w.n += 16
                block.gpsimd(gp)
            self.KV_AGV = self.s_cc.n
            self.WO_B = self.s_w.n

            def rope_path(which):
                BIA = self.BKB if which == "K" else self.BQB
                WGATE = BW0 if which == "K" else self.WQ_B
                rp0, rv0, rv20, rg0, rdk0 = (self.s_pe.n, self.s_ve.n, self.s_ve2.n,
                                             self.s_gp.n, self.s_dk.n)
                with nc.Block() as block:
                    def ten(t):
                        t.wait_ge(self.s_w.h, WGATE)
                        for c in range(8):
                            cb = c % 2
                            for tg in range(4):
                                m = 4 * c + tg
                                r = m % 2
                                if m >= 2:
                                    mm = m - 2
                                    t.wait_ge(self.s_ve.h, rv0 + 8 * (mm // 4) + (mm % 4) + 1)
                                for k in range(8):
                                    inst = t.matmul(PKB[r][:, :], WA[:, k, c * 128:(c + 1) * 128],
                                                    FM[:, k, tg * 512:(tg + 1) * 512],
                                                    start=(k == 0), stop=(k == 7))
                                inst.then_inc(self.s_pe.h, 1)
                                self.s_pe.n += 1
                            t.wait_ge(self.s_ve.h, rv0 + 8 * c + 4)
                            for q in range(4):
                                mq = 4 * c + q
                                r2 = mq % 2
                                if mq >= 2:
                                    mm = mq - 2
                                    t.wait_ge(self.s_ve.h, rv0 + 8 * (mm // 4) + (mm % 4) + 5)
                                inst = t.matmul(PKR[r2][:, :], PERM[:, :],
                                                RK[:, cb, q * 512:(q + 1) * 512],
                                                start=True, stop=True)
                                inst.then_inc(self.s_pe.h, 1)
                                self.s_pe.n += 1
                    block.tensor(ten)

                    def vec(v):
                        def front(c):
                            cb = c % 2
                            for tg in range(4):
                                r = (4 * c + tg) % 2
                                v.wait_ge(self.s_pe.h, rp0 + 8 * c + tg + 1)
                                v.tensor_scalar(RK[:, cb, tg * 512:(tg + 1) * 512], PKB[r][:, :],
                                                BIA[:, c:c + 1], None, ALU.add
                                                ).then_inc(self.s_ve.h, 1)
                                self.s_ve.n += 1
                            for q in range(4):
                                r2 = (4 * c + q) % 2
                                v.wait_ge(self.s_pe.h, rp0 + 8 * c + 4 + q + 1)
                                v.tensor_copy(RKS[:, cb, q * 512:(q + 1) * 512], PKR[r2][:, :]
                                              ).then_inc(self.s_ve.h, 1)
                                self.s_ve.n += 1
                            v.tensor_tensor(RKS[:, cb, :], RKS[:, cb, :], SIN[:, :], ALU.mult
                                            ).then_inc(self.s_ve2.h, 1)
                            self.s_ve2.n += 1

                        def tail(c):
                            cb = c % 2
                            v.wait_ge(self.s_gp.h, rg0 + c + 1)
                            if which == "K" and c >= 2:
                                v.wait_ge(self.s_dk.h, rdk0 + 16 * (c - 1))
                            dst = KBP[:, cb, :] if which == "K" else QB[:, c, :]
                            v.tensor_add(dst, RK[:, cb, :], RKS[:, cb, :]).then_inc(self.s_ve2.h, 1)
                            self.s_ve2.n += 1

                        for c in range(9):
                            if c < 8:
                                front(c)
                            if c >= 1:
                                tail(c - 1)
                    block.vector(vec)

                    def gp(g):
                        for c in range(8):
                            cb = c % 2
                            g.wait_ge(self.s_pe.h, rp0 + 8 * c + 8)
                            g.tensor_tensor(RK[:, cb, :], RK[:, cb, :], COS[:, :], ALU.mult
                                            ).then_inc(self.s_gp.h, 1)
                            self.s_gp.n += 1
                    block.gpsimd(gp)

                    if which == "K":
                        def syn(s):
                            for c in range(8):
                                cb = c % 2
                                s.wait_ge(self.s_ve2.h, rv20 + (2 * c + 3 if c < 7 else 16))
                                s.dma_start(
                                    out=self.kv_d[0].rearrange("(d t) -> d t", d=D)[c * 128:(c + 1) * 128, :],
                                    in_=KBP[:, cb, :]).then_inc(self.s_dk.h, 16)
                                self.s_dk.n += 16
                        block.sync(syn)

            rope_path("K")

            with nc.Block() as block:
                def gp(g, tgt=self.s_dk.n):
                    g.wait_ge(self.s_dk.h, tgt)
                    g.collective_compute("AllGather", ALU.bypass, replica_groups=GROUPS,
                                         ins=[self.kv_d[0]], outs=[self.kvK_o]).then_inc(self.s_cc.h, 1)
                    self.s_cc.n += 1
                    g.dma_start(out=WA[:, :, :],
                                in_=self.wqt_b.rearrange("(k p) o -> p k o", p=128)
                                ).then_inc(self.s_w.h, 16)
                    self.s_w.n += 16
                block.gpsimd(gp)
            self.KV_AGK = self.s_cc.n
            self.WQ_B = self.s_w.n

            rope_path("Q")


    def phaseB_attn(self):
        nc = self.nc
        with ExitStack() as ctx:
            esb = lambda *a: ctx.enter_context(nc.sbuf_tensor(*a))
            eps_ = lambda *a: ctx.enter_context(nc.psum_tensor(*a))
            KVK = esb("KVK", [128, 2, 2, T], BF16)
            KVV = esb("KVV", [128, 2, 2, 16, 128], BF16)
            PT = [esb(f"PTx{j}", [128, 1024], BF16) for j in range(3)]
            OSB = esb("OSB", [128, 512], F32)
            OSZ = esb("OSZ", [128, 512], F32)
            ST = [eps_(f"bst{j}", [128, 1024], F32) for j in range(3)]
            PV = eps_("bpv", [128, 512], F32)
            PZ = eps_("bpz", [128, 512], F32)
            QB, FM, ONES = self.QB, self.FM, self.ONES
            kK = self.kvK_o.rearrange("a (d t) -> a d t", d=D)
            kV = self.kvV_o.rearrange("a (t d) -> a t d", t=T)

            b_kv0, b_pe0, b_act0, b_ve0 = (self.s_kv.n, self.s_pe.n, self.s_act.n,
                                           self.s_ve.n)
            with nc.Block() as block:
                def syn(s):
                    s.wait_ge(self.s_cc.h, self.KV_AGK)
                    for hp in range(8):
                        hb = hp % 2
                        if hp >= 2:
                            s.wait_ge(self.s_ve.h, b_ve0 + 12 * (hp - 1))
                        if hp == 0:
                            s.wait_ge(self.s_cc.h, self.KV_AGV)
                        for kh in range(2):
                            s.dma_start(out=KVK[:, hb, kh, :],
                                        in_=kK[kh, hp * 128:(hp + 1) * 128, :]
                                        ).then_inc(self.s_kv.h, 16)
                            s.dma_start(out=KVV[:, hb, kh, :, :],
                                        in_=kV[kh, :, hp * 128:(hp + 1) * 128]
                                        .rearrange("(kb p) d -> p kb d", p=128)
                                        ).then_inc(self.s_kv.h, 16)
                            self.s_kv.n += 32
                block.sync(syn)

                def ten(t):
                    for hp in range(8):
                        hb = hp % 2
                        b_pe = b_pe0 + 256 * hp
                        b_act = b_act0 + 128 * hp
                        b_ve = b_ve0 + 12 * hp
                        t.wait_ge(self.s_kv.h, b_kv0 + 64 * (hp + 1))

                        def st_mm(ck, hb=hb, hp=hp, b_act=b_act):
                            qg, kb = ck // 32, ck % 32
                            kh, kbl = kb // 16, kb % 16
                            j = ck % 3
                            t.wait_ge(self.s_act.h, b_act + max(ck - 2, 0))
                            t.matmul(ST[j][:, 0:512],
                                     KVK[0:64, hb, kh, kbl * 128:(kbl + 1) * 128],
                                     QB[0:64, hp, qg * 512:(qg + 1) * 512],
                                     start=True, stop=True, tile_position=(0, 0))
                            inst = t.matmul(ST[j][:, 512:1024],
                                            KVK[64:128, hb, kh, kbl * 128:(kbl + 1) * 128],
                                            QB[64:128, hp, qg * 512:(qg + 1) * 512],
                                            start=True, stop=True, tile_position=(64, 0))
                            inst.then_inc(self.s_pe.h, 1)
                            self.s_pe.n += 1

                        def av_mm(ck, hb=hb, b_act=b_act, b_ve=b_ve):
                            qg, kb = ck // 32, ck % 32
                            kh, kbl = kb // 16, kb % 16
                            j = ck % 3
                            t.wait_ge(self.s_act.h, b_act + ck + 1)
                            if kb == 0:
                                t.wait_ge(self.s_ve.h, b_ve + 3 * qg)
                            st_, sp = (kb == 0), (kb == 31)
                            t.matmul(PV[0:64, :], KVV[:, hb, kh, kbl, 0:64], PT[j][:, 0:512],
                                     start=st_, stop=sp, tile_position=(0, 0))
                            t.matmul(PV[64:128, :], KVV[:, hb, kh, kbl, 64:128], PT[j][:, 512:1024],
                                     start=st_, stop=sp, tile_position=(0, 64))
                            t.matmul(PZ[0:64, :], ONES[:, 0:64], PT[j][:, 0:512],
                                     start=st_, stop=sp, tile_position=(0, 0))
                            inst = t.matmul(PZ[64:128, :], ONES[:, 64:128], PT[j][:, 512:1024],
                                            start=st_, stop=sp, tile_position=(0, 64))
                            inst.then_inc(self.s_pe.h, 1)
                            self.s_pe.n += 1

                        for ck in range(130):
                            if ck < 128:
                                st_mm(ck)
                            if ck >= 2:
                                av_mm(ck - 2)
                block.tensor(ten)

                def sca(sc):
                    for hp in range(8):
                        b_pe = b_pe0 + 256 * hp
                        for ck in range(128):
                            j = ck % 3
                            sc.wait_ge(self.s_pe.h, b_pe + (ck + 1 if ck < 3 else 2 * ck - 1))
                            sc.activation(PT[j][:, :], ST[j][:, :], AF.Exp,
                                          scale=0.125).then_inc(self.s_act.h, 1)
                            self.s_act.n += 1
                block.scalar(sca)

                def vec(v):
                    for hp in range(8):
                        b_pe = b_pe0 + 256 * hp
                        for qg in range(4):
                            v.wait_ge(self.s_pe.h, b_pe + min(2 * (32 * qg + 31) + 4, 256))
                            v.tensor_copy(OSB[:, :], PV[:, :]).then_inc(self.s_ve.h, 1)
                            v.reciprocal(OSZ[:, :], PZ[:, :]).then_inc(self.s_ve.h, 1)
                            self.s_ve.n += 2
                            v.wait_ge(self.s_ve.h, self.s_ve.n)
                            v.tensor_tensor(FM[:, hp, qg * 512:(qg + 1) * 512], OSB[:, :],
                                            OSZ[:, :], ALU.mult).then_inc(self.s_ve.h, 1)
                            self.s_ve.n += 1
                block.vector(vec)
            self.barrier()


        with ExitStack() as ctx:
            esb = lambda *a: ctx.enter_context(nc.sbuf_tensor(*a))
            eps_ = lambda *a: ctx.enter_context(nc.psum_tensor(*a))
            OST = esb("bOST", [128, 8, 512], F32)
            PO = [eps_(f"bpo{j}", [128, 512], F32) for j in range(2)]
            PT32 = eps_("bpt32", [128, 1024], F32)
            X, FM, WB, BOB, IDF = self.X, self.FM, self.WB, self.BOB, self.IDF
            for g in range(4):
                b_pe, b_ve, b_pb = self.s_pe.n, self.s_ve.n, self.s_pb.n
                with nc.Block() as block:
                    def ten(t, g=g, b_ve=b_ve, b_pb=b_pb):
                        for oc in range(8):
                            r = oc % 2
                            if oc >= 2:
                                t.wait_ge(self.s_ve.h, b_ve + oc - 1)
                            for hd in range(8):
                                inst = t.matmul(PO[r][:, :], WB[:, hd, oc * 128:(oc + 1) * 128],
                                                FM[:, hd, g * 512:(g + 1) * 512],
                                                start=(hd == 0), stop=(hd == 7))
                            inst.then_inc(self.s_pe.h, 1)
                            self.s_pe.n += 1
                        t.wait_ge(self.s_ve.h, b_ve + 8)
                        for sub in range(4):
                            if sub >= 1:
                                t.wait_ge(self.s_pb.h, b_pb + sub)
                            for oc in range(8):
                                inst = t.matmul(PT32[:, oc * 128:(oc + 1) * 128],
                                                OST[:, oc, sub * 128:(sub + 1) * 128],
                                                IDF[:, :], start=True, stop=True)
                            inst.then_inc(self.s_pe.h, 1)
                            self.s_pe.n += 1
                    block.tensor(ten)

                    def vec(v, g=g, b_pe=b_pe):
                        for oc in range(8):
                            r = oc % 2
                            v.wait_ge(self.s_pe.h, b_pe + oc + 1)
                            v.tensor_scalar(OST[:, oc, :], PO[r][:, :], BOB[:, oc:oc + 1], None,
                                            ALU.add).then_inc(self.s_ve.h, 1)
                            self.s_ve.n += 1
                        for sub in range(4):
                            i = g * 4 + sub
                            v.wait_ge(self.s_pe.h, b_pe + 8 + sub + 1)
                            v.tensor_add(X[:, i, :], X[:, i, :], PT32[:, :]).then_inc(self.s_pb.h, 1)
                            self.s_pb.n += 1
                    block.vector(vec)
            self.barrier()



    def p9_store(self):
        nc = self.nc
        with nc.Block() as block:
            def syn(s):
                xr = self.out_d.rearrange("(i p) d -> p i d", p=128)
                s.dma_start(out=xr, in_=self.X[:, :, :]).then_inc(self.s_st.h, 16)
                self.s_st.n += 16
                s.wait_ge(self.s_st.h, self.s_st.n)
            block.sync(syn)


def host_prep(inputs):
    x = np.asarray(inputs["L0_attn_a_x"])
    f32 = lambda a: np.ascontiguousarray(np.asarray(a), dtype=np.float32)
    bcol = lambda v: np.ascontiguousarray(np.asarray(v, np.float32).reshape(8, 128).T)

    com = {}
    com["wq_a"] = f32(inputs["L0_attn_a_Wq"])
    com["wk_a"] = f32(inputs["L0_attn_a_Wk"])
    com["wv_a"] = f32(inputs["L0_attn_a_Wv"])
    com["wo_a"] = f32(inputs["L0_attn_a_Wo"])
    com["boa_c"] = bcol(inputs["L0_attn_a_bo"])
    com["gamma_v"] = f32(inputs["L0_attn_a_gamma"])
    com["beta_v"] = f32(inputs["L0_attn_a_beta"])
    tb = lambda a: np.ascontiguousarray(np.asarray(a).T)
    com["wqt_b"] = tb(inputs["L0_attn_b_Wq"])
    com["wkt_b"] = tb(inputs["L0_attn_b_Wk"])
    com["wvt_b"] = tb(inputs["L0_attn_b_Wv"])
    com["wot_b"] = tb(inputs["L0_attn_b_Wo"])
    com["bqb_c"] = bcol(np.asarray(inputs["L0_attn_b_bq"], np.float32))
    com["bkb_c"] = bcol(np.asarray(inputs["L0_attn_b_bk"], np.float32))
    com["bvb_v"] = f32(inputs["L0_attn_b_bv"])
    com["bob_c"] = bcol(np.asarray(inputs["L0_attn_b_bo"], np.float32))
    win = f32(inputs["L0_ffn_Win"])
    com["windup"] = np.ascontiguousarray(np.concatenate([win, win], axis=1))
    wout = f32(inputs["L0_ffn_Wout"])
    com["woutdup"] = np.ascontiguousarray(np.concatenate([wout, wout], axis=0))
    w1 = f32(inputs["L0_ffn_W1"]); b1 = f32(inputs["L0_ffn_b1"]); w2 = f32(inputs["L0_ffn_W2"])
    dup = lambda a: np.concatenate([a, a], axis=0)
    com["ffncols"] = np.ascontiguousarray(
        np.concatenate([dup(w1), dup(b1), dup(w2)], axis=1))
    bin_ = f32(inputs["L0_ffn_bin_"]); b2 = f32(inputs["L0_ffn_b2"])
    com["ffnmisc"] = np.ascontiguousarray(
        np.stack([np.concatenate([bin_, bin_]), np.concatenate([b2, b2])], axis=1))
    com["boutc"] = bcol(inputs["L0_ffn_bout"])
    com["idb"] = np.eye(128, dtype=BF)
    com["idf"] = np.eye(128, dtype=np.float32)
    com["onesb"] = np.ones((128, 128), dtype=BF)
    perm = np.zeros((128, 128), dtype=BF)
    for p in range(128):
        src = (p // 64) * 64 + ((p % 64) + 32) % 64
        perm[src, p] = 1.0
    com["permb"] = perm
    com["epsv"] = np.array([1e-5, 0, 0, 0], np.float32)

    inv = 1.0 / (10000.0 ** (np.arange(0, 64, 2, dtype=np.float64) / 64.0))
    in_maps = []
    for c in range(8):
        b, h = c // 2, c % 2
        m = dict(com)
        m["x"] = np.ascontiguousarray(x[b, h * T:(h + 1) * T, :], dtype=np.float32)
        t = np.arange(h * T, (h + 1) * T, dtype=np.float64)
        ang = np.outer(inv, t)
        cos64 = np.concatenate([np.cos(ang), np.cos(ang)], axis=0)
        sin32 = np.sin(ang)
        sinS64 = np.concatenate([-sin32, sin32], axis=0)
        m["cosT"] = np.ascontiguousarray(np.concatenate([cos64, cos64], axis=0).astype(BF))
        m["sinS"] = np.ascontiguousarray(np.concatenate([sinS64, sinS64], axis=0).astype(BF))
        in_maps.append(m)
    return in_maps


_CACHE = {}


def get_nc(stages=4):
    if stages not in _CACHE:
        _CACHE[stages] = Builder(stages).build()
    return _CACHE[stages]


def kernel(**inputs):
    in_maps = host_prep(inputs)
    nc = get_nc(4)
    res = run_bass_kernel_spmd(nc, in_maps, core_ids=list(range(8)))
    x = np.asarray(inputs["L0_attn_a_x"])
    out = np.empty((4, 4096, 1024), dtype=np.float32)
    for c in range(8):
        b, h = c // 2, c % 2
        out[b, h * T:(h + 1) * T, :] = res.results[c]["out"]
    return out
